# version 22
# baseline (speedup 1.0000x reference)
import contextlib
from collections import defaultdict

import numpy as np

import concourse.bass as bass
import concourse.mybir as mybir
from concourse.bass_utils import run_bass_kernel_spmd

F32 = mybir.dt.float32
BF16 = mybir.dt.float16
AF = mybir.ActivationFunctionType
ALU = mybir.AluOpType

DEBUG = False
SERIAL2 = False

B, N, D, H, HD = 2, 8192, 1024, 16, 64
TOK, HALO, TOT = 2048, 1536, 3584
NT = TOT // 128
NQT = TOK // 128
NBLK = 7
NSLOT = 46
CW = NSLOT * 16

DENSE = list(range(33))
DYAD = [48, 64, 96, 128, 192, 256, 384, 512, 768, 1024, 1536]
OFFSETS = DENSE + DYAD
OIDX = {o: i for i, o in enumerate(OFFSETS)}
SLOT_OFF = [32 - s for s in range(33)] + [128, 112, 96, 80, 64, 48] + \
           [256, 192] + [512, 384] + [768] + [1024] + [1536]
SLOT_REAL = [True] * 33 + [True, False, True, False, True, True] + [True] * 7

GROUPS = [
    (-32, 160, 0, 33, 1),
    (-128, 208, 33, 6, 16),
    (-256, 192, 39, 2, 64),
]
EWS = [(41, 8), (42, 9), (43, 6), (44, 4), (45, 0)]
RING = 10
NKT = 25
KB0 = 1280
KW = TOT - KB0
NG = len(GROUPS)
NGEN = NG * 2
NGEN4 = NG * 4
BO = [-2, -1, 0]
NB = len(BO)
DY_ROW0 = {}
for o in DYAD:
    bo_base = -((o + 127) // 128)
    if bo_base in BO:
        DY_ROW0[o] = BO.index(bo_base) * 128 - o - bo_base * 128
LOC_ROW0 = BO.index(-1) * 128 - 32 + 128

ENGINES = ("sync", "tensor", "vector", "scalar", "gpsimd")
DMASEMS = ("SQ", "AQ", "GQ")


class Sched:
    def __init__(self):
        self.q = {e: [] for e in ENGINES}
        self.cnt = defaultdict(int)
        self.last = {e: None for e in ENGINES}
        self.batch = {}
        self.nbatch = 0

    def op(self, eng, fn, waits=(), inc=None, batch=None):
        tok = None
        pre = None
        if inc is not None:
            sem, amt = inc
            if sem in DMASEMS:
                if batch is None:
                    self.nbatch += 1
                    batch = ("_auto", self.nbatch)
                if self.batch.get(sem) != batch:
                    pre = (sem, self.cnt[sem])
                    self.batch[sem] = batch
            self.cnt[sem] += amt
            tok = (sem, self.cnt[sem])
        ws = [(sm, self.cnt[sm] if sm in DMASEMS else v)
              for (sm, v) in waits if (sm, v) is not None and v is not None
              ] if False else []
        for w in waits:
            if w is None:
                continue
            sm, v = w
            ws.append((sm, self.cnt[sm] - (16 if w is tok else 0)) if sm in DMASEMS else (sm, v))
        if pre is not None:
            ws.append(pre)
        is_dma = inc is not None and inc[0] in DMASEMS
        if eng != "tensor" and not is_dma and self.last[eng] is not None:
            ws.append(self.last[eng])
        if tok is not None and not is_dma:
            self.last[eng] = tok
        self.q[eng].append((fn, ws, tok))
        return tok

    def barrier(self):
        now = [(s, self.cnt[s]) for s in ("TS", "VS", "SS", "GS") + DMASEMS if self.cnt[s] > 0]
        toks = []
        for e in ENGINES:
            toks.append(self.op(e, None, now, None))
        return toks


def emit(nc, sched, sems):
    with nc.Block() as block:
        for eng in ENGINES:
            ops = sched.q[eng]

            def make_body(ops):
                def body(e):
                    seen = {}
                    for fn, waits, tok in ops:
                        for (s, v) in waits:
                            if seen.get(s, -1) < v:
                                e.wait_ge(sems[s], v)
                                seen[s] = v
                        if fn is None:
                            if tok is not None:
                                e.nop(nofuse=True).then_inc(sems[tok[0]], 1)
                            continue
                        instr = fn(e)
                        if tok is not None:
                            sem, _ = tok
                            instr.then_inc(sems[sem], 16 if sem in DMASEMS else 1)
                return body
            getattr(block, eng)(make_body(ops))


def build_program():
    nc = bass.Bass("TRN2", target_bir_lowering=False)
    t = {}

    def din(name, shape, dtype=F32):
        t[name] = nc.dram_tensor(name, shape, dtype, kind="ExternalInput")

    def dout(name, shape, dtype=F32):
        t[name] = nc.dram_tensor(name, shape, dtype, kind="ExternalOutput")

    def dint(name, shape, dtype=BF16):
        t[name] = nc.dram_tensor(name, shape, dtype, kind="Internal")

    din("xs", [TOT, D]); din("wqkvT", [D, 3 * D]); din("wgT", [D, D]); din("woT", [D, D])
    din("qkvb", [3 * D]); din("gateb", [D]); din("outb", [D]); din("ffnb", [D])
    din("se2", [128, NSLOT]); din("pbm", [NQT, 128, CW], BF16); din("ifg", [16])
    din("ifgcolh", [128, 8])
    din("g1rh", [4, 4, 128, 32]); din("g2rh", [4, 128, 16])
    din("g1ih", [4, 4, 128, 32]); din("g2ih", [4, 128, 16])
    din("ident", [128, 128])
    dout("out", [TOK, D])
    if DEBUG:
        dout("dbg_kfm", [128, 8 * TOT], BF16); dout("dbg_vt", [128, NT * 1024], BF16)
        dout("dbg_y1", [128, 8 * TOT], BF16); dout("dbg_of", [128, 8 * TOK], BF16)
        dout("dbg_band", [NQT, 128, CW], BF16); dout("dbg_qs", [NQT, 128, CW], BF16)
        dout("dbg_P", [NQT, 128, CW], BF16); dout("dbg_lg", [NQT, 128, CW])
        dout("dbg_x1", [TOK, D]); dout("dbg_psp", [NB * 128, 2048], BF16)

    dint("q_img", [8 * 128 * TOK]); dint("gate_img", [8 * 128 * TOK])
    dint("x1_img", [TOK, D], F32)
    for gi, (w0o, w, sb_, ns, ds) in enumerate(GROUPS):
        dint(f"rimg{gi}", [2 * 2 * 128 * (8 * w + 8)])
    dint("psp_img", [2 * NB * 128 * 2048])
    dint("ktok", [NKT * 128 * 1024])
    dint("qtok", [NQT * 128 * 1024])

    sched = Sched()
    S = sched.op

    es_all = contextlib.ExitStack()
    sems = {}
    for sname in DMASEMS + ("TS", "VS", "SS", "GS") + tuple("bar_" + e for e in ENGINES):
        sems[sname] = es_all.enter_context(nc.semaphore(sname))

    def sb(name, shape, dtype, stack):
        return stack.enter_context(nc.sbuf_tensor(name, shape, dtype))

    def psum(name, shape, dtype, stack):
        return stack.enter_context(nc.psum_tensor(name, shape, dtype))

    es0 = es_all
    se2b = sb("se2b", [128, NSLOT], BF16, es0)
    idb = sb("idb", [128, 128], BF16, es0)
    kbcol = sb("kbcol", [128, 8], F32, es0)
    qbcol = sb("qbcol", [128, 8], F32, es0)
    gbcol = sb("gbcol", [128, 8], F32, es0)
    obcol = sb("obcol", [128, 8], F32, es0)
    fbcol = sb("fbcol", [128, 8], F32, es0)
    bvifg = sb("bvifg", [128, 8], F32, es0)
    epsc = sb("epsc", [128, 1], F32, es0)

    with contextlib.ExitStack() as es:
        se2f = sb("se2f", [128, NSLOT], F32, es)
        idf = sb("idf", [128, 128], F32, es)
        bvcol = sb("bvcol", [128, 8], F32, es)
        ifgcol = sb("ifgcol", [128, 8], F32, es)
        zzb = sb("zzb", [128, 2048], BF16, es)

        qb = t["qkvb"]
        tl = S("gpsimd", lambda e: e.dma_start(out=se2f[:], in_=t["se2"].ap()), (), ("GQ", 16))
        ti = S("gpsimd", lambda e: e.dma_start(out=idf[:], in_=t["ident"].ap()), (), ("GQ", 16))
        tb1 = S("gpsimd", lambda e: e.dma_start(out=kbcol[:], in_=bass.AP(qb, 1024, [[1, 128], [128, 8]])), (), ("GQ", 16))
        tb2 = S("gpsimd", lambda e: e.dma_start(out=qbcol[:], in_=bass.AP(qb, 0, [[1, 128], [128, 8]])), (), ("GQ", 16))
        tb3 = S("gpsimd", lambda e: e.dma_start(out=bvcol[:], in_=bass.AP(qb, 2048, [[1, 128], [128, 8]])), (), ("GQ", 16))
        tb4 = S("gpsimd", lambda e: e.dma_start(out=gbcol[:], in_=bass.AP(t["gateb"], 0, [[1, 128], [128, 8]])), (), ("GQ", 16))
        tb5 = S("gpsimd", lambda e: e.dma_start(out=obcol[:], in_=bass.AP(t["outb"], 0, [[1, 128], [128, 8]])), (), ("GQ", 16))
        tb6 = S("gpsimd", lambda e: e.dma_start(out=fbcol[:], in_=bass.AP(t["ffnb"], 0, [[1, 128], [128, 8]])), (), ("GQ", 16))
        tb7 = S("gpsimd", lambda e: e.dma_start(out=ifgcol[:], in_=t["ifgcolh"].ap()), (), ("GQ", 16))

        c1 = S("vector", lambda e: e.tensor_copy(se2b[:], se2f[:]), (tl,), ("VS", 1))
        c2 = S("vector", lambda e: e.tensor_copy(idb[:], idf[:]), (ti,), ("VS", 1))
        c3 = S("vector", lambda e: e.tensor_mul(bvifg[:], bvcol[:], ifgcol[:]), (tb3, tb7), ("VS", 1))
        zz = S("vector", lambda e: e.memset(zzb[:], 0.0), (), ("VS", 1))
        S("vector", lambda e: e.memset(epsc[:], 1e-5), (), ("VS", 1))
        last_pz = None
        for half in range(2):
            for ib in range(NB):
                off = half * NB * 128 * 2048 + ib * 128 * 2048
                last_pz = S("sync", lambda e, _o=off: e.dma_start(
                    out=bass.AP(t["psp_img"], _o, [[2048, 128], [1, 2048]]), in_=zzb[:]), (zz,), ("SQ", 16))
        sched.barrier()

    esOF = contextlib.ExitStack()
    outfl = sb("outfl", [128, 8 * TOK], BF16, esOF)
    es1 = contextlib.ExitStack()
    kfm = sb("kfm", [128, 8 * KW], BF16, es1)
    vtok = sb("vtok", [128, NT * 1024], BF16, es1)
    es1y = contextlib.ExitStack()
    y1fm = sb("y1fm", [128, 8 * TOT], BF16, es1y)
    y1v = y1fm[:].rearrange("p (c tq) -> p c tq", c=8)

    with contextlib.ExitStack() as e1a:
        xt = [sb("xt0", [128, D], F32, e1a)] * 2
        y1t = [sb("y1t0", [128, D], BF16, e1a)] * 2
        sumc = sb("sumc", [128, 1], F32, e1a)
        negm = sb("negm", [128, 1], F32, e1a)
        ssq = sb("ssq", [128, 1], F32, e1a)
        rstd = sb("rstd", [128, 1], F32, e1a)
        nmr = sb("nmr", [128, 1], F32, e1a)
        trp = [psum(f"trp{i}", [128, 1024], BF16, e1a) for i in range(2)]

        ln_tok = [None, None]
        ev_tok = [None, None]
        tps_tok = [None, None]
        for tt in range(NT):
            bs = tt % 2
            ld = S("gpsimd", lambda e, _o=xt[bs][:], _i=bass.AP(t["xs"], tt * 128 * D, [[D, 128], [1, D]]):
                   e.dma_start(out=_o, in_=_i), (ln_tok[1],), ("GQ", 16))
            v1 = S("vector", lambda e, _x=xt[bs][:]: e.tensor_reduce(sumc[:], _x, mybir.AxisListType.X, ALU.add), (ld,), ("VS", 1))
            v2 = S("vector", lambda e: e.tensor_scalar(negm[:], sumc[:], -1.0 / D, None, ALU.mult), (v1,), ("VS", 1))
            s1 = S("scalar", lambda e, _x=xt[bs][:], _y=y1t[bs][:]: e.activation(_y, _x, AF.Square, bias=negm[:], scale=1.0, accum_out=ssq[:]), (v2, tps_tok[1]), ("SS", 1))
            s2a = S("scalar", lambda e: e.activation(rstd[:], ssq[:], AF.Sqrt, bias=epsc[:], scale=1.0 / D), (s1,), ("SS", 1))
            s2 = S("vector", lambda e: e.reciprocal(rstd[:], rstd[:]), (s2a,), ("VS", 1))
            v3 = S("vector", lambda e: e.tensor_mul(nmr[:], negm[:], rstd[:]), (s2,), ("VS", 1))
            s3 = S("scalar", lambda e, _x=xt[bs][:], _y=y1t[bs][:]:
                   e.activation(_y, _x, AF.Identity, bias=nmr[:], scale=rstd[:]), (v3,), ("SS", 1))
            ln_tok[1] = s3
            tps = None
            for c in range(8):
                tps = S("tensor", lambda e, _o=trp[bs][:, c * 128:(c + 1) * 128],
                        _i=y1t[bs][:, c * 128:(c + 1) * 128]:
                        e.transpose(_o, _i, idb[:]), (s3, ev_tok[bs], c2), ("TS", 1))
            tps_tok[1] = tps
            ev = S("vector", lambda e, _o=y1v[:, :, tt * 128:(tt + 1) * 128],
                   _i=trp[bs][:].rearrange("p (c tq) -> p c tq", c=8):
                   e.tensor_copy(_o, _i), (tps,), ("VS", 1))
            ev_tok[bs] = ev
        sched.barrier()

    if DEBUG:
        S("sync", lambda e: e.dma_start(out=t["dbg_y1"].ap(), in_=y1fm[:]), (), ("SQ", 16))

    with contextlib.ExitStack() as e1k:
        ifgcol = sb("ifgcol2", [128, 8], F32, e1k)
        wcol = [sb(f"wcol{i}", [128, 8 * 128], BF16, e1k) for i in range(2)]
        qtmp = [sb("qtmp0", [128, 512], BF16, e1k)] * 2
        vtmp = [sb("vtmp0", [128, 512], BF16, e1k)] * 2
        qts = sb("qts", [128, 512], BF16, e1k)
        kearly = sb("kearly", [128, 8 * KB0], BF16, e1k)
        kqps = [psum(f"kqps{i}", [128, 512], F32, e1k) for i in range(2)]
        trpv = [psum(f"trpv{i}", [128, 512], BF16, e1k) for i in range(2)]
        trpk = [psum(f"trpk{i}", [128, 1024], BF16, e1k) for i in range(2)]
        S("gpsimd", lambda e: e.dma_start(out=ifgcol[:], in_=t["ifgcolh"].ap()), (), ("GQ", 16))
        jobs = [(0, oc) for oc in range(8)] + [(3, oc) for oc in range(8)] + \
               [(1, oc) for oc in range(8)] + [(2, oc) for oc in range(8)]
        wl_tok = [None, None]
        prev_job_last_ev = None
        last_k_ev = None
        qtp_tok = {}
        last_qts_ev = None
        last_qts_dma = None
        sp_tok = {}
        vev_tok = {}
        vtp_tok = {}
        for ji, (kind, oc) in enumerate(jobs):
            ws = ji % 2
            if kind == 0:
                wsrc = bass.AP(t["wqkvT"], 1024 + oc * 128, [[3072, 128], [128 * 3072, 8], [1, 128]])
            elif kind == 1:
                wsrc = bass.AP(t["wqkvT"], oc * 128, [[3072, 128], [128 * 3072, 8], [1, 128]])
            elif kind == 3:
                wsrc = bass.AP(t["wqkvT"], 2048 + oc * 128, [[3072, 128], [128 * 3072, 8], [1, 128]])
            else:
                wsrc = bass.AP(t["wgT"], oc * 128, [[1024, 128], [128 * 1024, 8], [1, 128]])
            wl = S("gpsimd", lambda e, _o=wcol[ws][:], _i=wsrc: e.dma_start(out=_o, in_=_i),
                   (wl_tok[ws],), ("GQ", 16))
            tbs = list(range(NBLK)) if kind in (0, 3) else list(range(3, NBLK))
            job_ev = {}
            for tb in tbs:
                pslot = tb % 2
                mm = None
                for ic in range(8):
                    mov = y1v[:, ic, tb * 512:(tb + 1) * 512]
                    mm = S("tensor", lambda e, _s=wcol[ws][:, ic * 128:(ic + 1) * 128], _m=mov,
                           _o=kqps[pslot][:], _st=(ic == 0), _sp=(ic == 7):
                           e.matmul(_o, _s, _m, start=_st, stop=_sp),
                           (wl, job_ev.get(tb - 2), prev_job_last_ev), ("TS", 1))
                if kind == 0:
                    if tb >= 3:
                        evo = kfm[:, oc * KW + tb * 512 - KB0: oc * KW + (tb + 1) * 512 - KB0]
                        ev = S("scalar", lambda e, _o=evo, _i=kqps[pslot][:], _b=kbcol[:, oc:oc + 1]:
                               e.activation(_o, _i, AF.Identity, bias=_b), (mm,), ("SS", 1))
                    elif tb < 2:
                        evo = kearly[:, oc * KB0 + tb * 512: oc * KB0 + (tb + 1) * 512]
                        ev = S("scalar", lambda e, _o=evo, _i=kqps[pslot][:], _b=kbcol[:, oc:oc + 1]:
                               e.activation(_o, _i, AF.Identity, bias=_b), (mm,), ("SS", 1))
                    else:
                        S("scalar", lambda e, _o=kearly[:, oc * KB0 + 1024: oc * KB0 + 1280],
                          _i=kqps[pslot][:, 0:256], _b=kbcol[:, oc:oc + 1]:
                          e.activation(_o, _i, AF.Identity, bias=_b), (mm,), ("SS", 1))
                        ev = S("scalar", lambda e, _o=kfm[:, oc * KW: oc * KW + 256],
                               _i=kqps[pslot][:, 256:512], _b=kbcol[:, oc:oc + 1]:
                               e.activation(_o, _i, AF.Identity, bias=_b), (mm,), ("SS", 1))
                    last_k_ev = ev
                elif kind == 3:
                    vs_ = tb % 2
                    ev = S("scalar", lambda e, _o=vtmp[vs_][:], _i=kqps[pslot][:], _sc=ifgcol[:, oc:oc + 1]:
                           e.activation(_o, _i, AF.Identity, scale=_sc), (mm, vtp_tok.get((ji, tb - 1)), vtp_tok.get((ji - 1, NBLK - 1))), ("SS", 1))
                    tps = None
                    for ts4 in range(4):
                        tps = S("tensor", lambda e, _o=trpv[vs_][:, ts4 * 128:(ts4 + 1) * 128],
                                _i=vtmp[vs_][:, ts4 * 128:(ts4 + 1) * 128]:
                                e.transpose(_o, _i, idb[:]), (ev, vev_tok.get((ji, tb - 2))), ("TS", 1))
                    vtp_tok[(ji, tb)] = tps
                    vdst = bass.AP(vtok[:].tensor, (tb * 4) * 1024 + oc * 128,
                                   [[NT * 1024, 128], [1024, 4], [1, 128]])
                    vev = S("vector", lambda e, _o=vdst,
                            _i=trpv[vs_][:].rearrange("p (a j) -> p a j", a=4):
                            e.tensor_copy(_o, _i), (tps,), ("VS", 1))
                    vev_tok[(ji, tb)] = vev
                    ev = vev
                else:
                    tbo = tb - 3
                    qs_ = tbo % 2
                    bias = qbcol[:, oc:oc + 1] if kind == 1 else gbcol[:, oc:oc + 1]
                    fn = AF.Identity if kind == 1 else AF.Sigmoid
                    ev = S("scalar", lambda e, _o=qtmp[qs_][:], _i=kqps[pslot][:], _b=bias, _f=fn:
                           e.activation(_o, _i, _f, bias=_b), (mm, sp_tok.get((ji, tbo - 1)), sp_tok.get((ji - 1, 3)),
                                                              qtp_tok.get((ji, tbo - 2))), ("SS", 1))
                    img = t["q_img"] if kind == 1 else t["gate_img"]
                    sp = S("sync", lambda e, _i=qtmp[qs_][:],
                           _o=bass.AP(img, oc * 128 * TOK + tbo * 512, [[TOK, 128], [1, 512]]):
                           e.dma_start(out=_o, in_=_i), (ev,), ("SQ", 16))
                    sp_tok[(ji, tbo)] = sp
                    if kind == 1:
                        tpq = None
                        for a4 in range(4):
                            tpq = S("tensor", lambda e, _o=trpk[0][:, a4 * 128:(a4 + 1) * 128],
                                    _i=qtmp[qs_][:, a4 * 128:(a4 + 1) * 128]:
                                    e.transpose(_o, _i, idb[:]),
                                    (ev, last_qts_ev), ("TS", 1))
                        qtp_tok[(ji, tbo)] = tpq
                        qeng = "vector" if (oc + tbo) % 2 == 0 else "scalar"
                        qcp = (lambda e: e.tensor_copy(qts[:], trpk[0][:, 0:512]))                             if qeng == "vector" else                               (lambda e: e.copy(qts[:], trpk[0][:, 0:512]))
                        last_qts_ev = S(qeng, qcp, (tpq, last_qts_dma),
                                        ("VS" if qeng == "vector" else "SS", 1))
                        last_qts_dma = S("sync", lambda e,
                                         _o=bass.AP(t["qtok"], (tbo * 4) * 128 * 1024 + oc * 128,
                                                    [[1024, 128], [128 * 1024, 4], [1, 128]]),
                                         _i=bass.AP(qts[:].tensor, 0, [[512, 128], [128, 4], [1, 128]]):
                                         e.dma_start(out=_o, in_=_i),
                                         (last_qts_ev,), ("SQ", 16), batch=("qtw", (oc, tbo)))
                job_ev[tb] = ev
            prev_job_last_ev = job_ev[tbs[-1]]
            wl_tok[ws] = prev_job_last_ev
        ktv_ev = {}
        ktv_dma = {}
        for T in range(NKT):
            tslot = T % 2
            tps = None
            for oc in range(8):
                if T >= 10:
                    src = kfm[:, oc * KW + T * 128 - KB0: oc * KW + (T + 1) * 128 - KB0]
                else:
                    src = kearly[:, oc * KB0 + T * 128: oc * KB0 + (T + 1) * 128]
                tps = S("tensor", lambda e, _o=trpk[tslot][:, oc * 128:(oc + 1) * 128], _i=src:
                        e.transpose(_o, _i, idb[:]),
                        (last_k_ev, last_qts_ev, ktv_ev.get(T - 2)), ("TS", 1))
            eng = "vector" if T % 2 == 0 else "scalar"
            cpf = (lambda e, _o=wcol[tslot][:], _i=trpk[tslot][:]: e.tensor_copy(_o, _i)) \
                if eng == "vector" else \
                  (lambda e, _o=wcol[tslot][:], _i=trpk[tslot][:]: e.copy(_o, _i))
            ev = S(eng, cpf, (tps, prev_job_last_ev, ktv_dma.get(T - 2)),
                   ("VS" if eng == "vector" else "SS", 1))
            ktv_ev[T] = ev
            ktv_dma[T] = S("sync", lambda e, _o=bass.AP(t["ktok"], T * 128 * 1024, [[1024, 128], [1, 1024]]),
                           _i=wcol[tslot][:]: e.dma_start(out=_o, in_=_i),
                           (ev,), ("SQ", 16), batch=("ktw", T))
        sched.barrier()

    if DEBUG:
        S("sync", lambda e: e.dma_start(out=t["dbg_kfm"].ap(), in_=kfm[:]), (), ("SQ", 16))
        S("sync", lambda e: e.dma_start(out=t["dbg_vt"].ap(), in_=vtok[:]), (), ("SQ", 16))

    es1y.close()

    es2 = contextlib.ExitStack()
    pspsb = sb("pspsb", [128, NB * 2048], BF16, es2)
    band = [sb(f"band{i}", [128, CW], BF16, es2) for i in range(2)]
    qsb = [sb(f"qsb{i}", [128, CW], BF16, es2) for i in range(2)]
    qsl = [sb(f"qsl{i}", [128, 1024], BF16, es2) for i in range(2)]
    pbmb = [sb(f"pbmb{i}", [128, CW], BF16, es2) for i in range(2)]
    rbuf = [sb(f"rbuf{i}", [128, 8 * 208], BF16, es2) for i in range(2)]
    t2b = sb("t2b", [128, CW], F32, es2)
    Pb = [sb(f"Pb{i}", [128, CW], BF16, es2) for i in range(2)]
    Zb = sb("Zb", [128, 16], F32, es2)
    rb = sb("rb", [128, 16], F32, es2)
    ktr = sb("ktr", [128, RING * 1024], BF16, es2)
    qtb = sb("qtb", [128, 1024], BF16, es2)
    scr = sb("scr", [128, 1024], BF16, es2)
    acch = sb("acch", [128, 1024], BF16, es2)
    accTb = sb("accTb", [128, 1024], BF16, es2)
    ptmp = sb("ptmp", [128, 1024], BF16, es2)
    pvv = sb("pvv", [128, 1024], BF16, es2)
    ptv = sb("ptv", [128, 1024], BF16, es2)
    bandx = sb("bandx", [128, 80], F32, es2)
    genps = psum("genps", [128, 2048], F32, es2)
    pvps = [psum(f"pvps{i}", [128, 512], F32, es2) for i in range(2)]
    pvpsB = [psum(f"pvpsB{i}", [128, 512], F32, es2) for i in range(2)]

    qsl_tok = [None, None]; pbm_tok = [None, None]
    gen_mm = {}; gen_ev = {}; gen_free = {}; rw_tok = {}; brd_tok = {}
    psp_rd = {}; psp_wr = {}; pv_tok = {}; pvev = {}
    P_tok = {}; exp_tok = {}; sm2 = {}
    band_used = {}; qs_used = {}; pbm_used = {}
    last_gen = {}
    qtb_ev = {}; ew_lastmul = {}; ew_red = {}; ring_ld = {}
    pve_last = {}; pve_cmb = {}; accT_tok = {}; merge_tok = {}

    for T in range(RING):
        ring_ld[T] = S("gpsimd", lambda e, _o=ktr[:, (T % RING) * 1024:(T % RING + 1) * 1024],
                       _i=bass.AP(t["ktok"], T * 128 * 1024, [[1024, 128], [1, 1024]]):
                       e.dma_start(out=_o, in_=_i), (), ("GQ", 16), batch=("kr", -1))

    def stage_C_hp(qt, hp):
        pslot = hp % 2
        mm = None
        for ii, ib in enumerate(range(NB)):
            tile_ib = 12 + qt + BO[ib]
            for ab in range(2):
                h = 2 * hp + ab
                stat = vtok[:, tile_ib * 1024 + h * 64: tile_ib * 1024 + (h + 1) * 64]
                mov = bass.AP(pspsb[:].tensor, ib * 2048 + h, [[NB * 2048, 128], [16, 128]])
                outap = pvps[pslot][0:64, 0:128] if ab == 0 else pvpsB[pslot][64:128, 0:128]
                mm = S("tensor", lambda e, _o=outap, _s=stat, _m=mov, _st=(ii == 0), _sp=(ii == NB - 1),
                       _tp=((0, 0) if ab == 0 else (0, 64)):
                       e.matmul(_o, _s, _m, start=_st, stop=_sp, tile_position=_tp),
                       (psp_rd.get(qt), pvev.get((qt, hp - 2)), pvev.get((qt - 1, hp + 6))), ("TS", 1))
        pv_tok[(qt, hp)] = mm
        ev = S("scalar", lambda e, _o=outfl[0:64, hp * TOK + qt * 128: hp * TOK + (qt + 1) * 128],
               _i=pvps[pslot][0:64, 0:128], _b=bvifg[0:64, hp:hp + 1]:
               e.activation(_o, _i, AF.Identity, bias=_b), (mm,), ("SS", 1))
        ev = S("scalar", lambda e, _o=outfl[64:128, hp * TOK + qt * 128: hp * TOK + (qt + 1) * 128],
               _i=pvpsB[pslot][64:128, 0:128], _b=bvifg[64:128, hp:hp + 1]:
               e.activation(_o, _i, AF.Identity, bias=_b), (mm,), ("SS", 1))
        pvev[(qt, hp)] = ev
        merge_tok[qt] = S("vector", lambda e,
                          _o=outfl[:, hp * TOK + qt * 128: hp * TOK + (qt + 1) * 128],
                          _a=accTb[:, hp * 128:(hp + 1) * 128]:
                          e.tensor_tensor(_o, _o, _a, ALU.add),
                          (ev, accT_tok[qt]), ("VS", 1))

    def stage_A(qt):
        n0 = HALO + qt * 128
        slot = qt % 2
        qsl_tok[slot] = S("scalar", lambda e, _o=qsl[slot][:],
                          _i=bass.AP(t["q_img"], qt * 128, [[TOK, 128], [128 * TOK, 8], [1, 128]]):
                          e.dma_start(out=_o, in_=_i), (last_gen.get(qt - 2),), ("AQ", 16), batch=("ld", qt))
        pbm_tok[slot] = S("scalar", lambda e, _o=pbmb[slot][:],
                          _i=bass.AP(t["pbm"], qt * 128 * CW, [[CW, 128], [1, CW]]):
                          e.dma_start(out=_o, in_=_i), (pbm_used.get(qt - 2),), ("AQ", 16), batch=("ld", qt))
        qtb_ev[qt] = S("scalar", lambda e, _o=qtb[:],
                       _i=bass.AP(t["qtok"], qt * 128 * 1024, [[1024, 128], [1, 1024]]):
                       e.dma_start(out=_o, in_=_i),
                       (ew_lastmul.get(qt - 1),), ("AQ", 16), batch=("ld", qt))
        genidx = 0
        rwidx = 0
        pvi = [0]
        for gi, (w0o, w, sb_, ns, ds) in enumerate(GROUPS):
            w0 = n0 + w0o
            for quarter in range(4):
                half = quarter // 2
                pbank = (genidx % 2) * 1024
                mm = None
                for hs in range(4):
                    h = quarter * 4 + hs
                    hp, r = h // 2, h % 2
                    reg = pbank + (hs % 2) * 512 + (hs // 2) * 256
                    stat = qsl[slot][r * 64:(r + 1) * 64, hp * 128:(hp + 1) * 128]
                    mov = kfm[r * 64:(r + 1) * 64, hp * KW + w0 - KB0: hp * KW + w0 - KB0 + w]
                    waits = [qsl_tok[slot], gen_free.get((qt, genidx - 2)),
                             (gen_free.get((qt - 1, NGEN4 - 2 + (genidx % 2))) if genidx < 2 else None)]
                    mm = S("tensor", lambda e, _o=genps[:, reg:reg + w], _s=stat, _m=mov, _tp=(r * 64, 0):
                           e.matmul(_o, _s, _m, start=True, stop=True, tile_position=_tp), waits, ("TS", 1))
                    if gi == 0:
                        mm = S("tensor", lambda e, _o=genps[:, reg + 160:reg + 160 + NSLOT], _s=stat,
                               _m=se2b[r * 64:(r + 1) * 64, :], _tp=(r * 64, 0):
                               e.matmul(_o, _s, _m, start=True, stop=True, tile_position=_tp),
                               (c1,), ("TS", 1))
                gen_mm[(qt, genidx)] = mm
                if qt >= 2 and genidx % 3 != 2 and pvi[0] < 8:
                    stage_C_hp(qt - 2, pvi[0])
                    pvi[0] += 1
                eng = "vector" if genidx % 2 == 0 else "scalar"
                esem = "VS" if eng == "vector" else "SS"
                rslot = rwidx % 2
                pin = bass.AP(genps[:].tensor, pbank, [[2048, 128], [256, 2], [512, 2], [1, w]])
                pout = bass.AP(rbuf[rslot][:].tensor, (quarter % 2) * 4, [[8 * 208, 128], [2, 2], [1, 2], [8, w]])
                cpf = (lambda e, _o=pout, _i=pin: e.tensor_copy(_o, _i)) if eng == "vector" else \
                      (lambda e, _o=pout, _i=pin: e.copy(_o, _i))
                ev = S(eng, cpf,
                       (mm, rw_tok.get((qt, rwidx - 2)), rw_tok.get((qt - 1, rwidx + 4))), (esem, 1))
                gen_ev[(qt, genidx)] = ev
                freetok = ev
                if gi == 0:
                    qi = bass.AP(genps[:].tensor, pbank + 160, [[2048, 128], [256, 2], [512, 2], [1, NSLOT]])
                    qo = bass.AP(qsb[slot][:].tensor, quarter * 4, [[CW, 128], [2, 2], [1, 2], [16, NSLOT]])
                    qcpf = (lambda e, _o=qo, _i=qi: e.tensor_scalar(_o, _i, 1.0, None, ALU.add)) \
                        if eng == "vector" else \
                           (lambda e, _o=qo, _i=qi: e.activation(_o, _i, AF.Identity, bias=1.0))
                    freetok = S(eng, qcpf, (mm, qs_used.get(qt - 2)), (esem, 1))
                gen_free[(qt, genidx)] = freetok
                genidx += 1
                if quarter % 2 == 1:
                    roff = ((qt % 2) * 2 + half) * 128 * (8 * w + 8)
                    rw = S("sync", lambda e, _i=rbuf[rslot][:, 0:8 * w],
                           _o=bass.AP(t[f"rimg{gi}"], roff, [[8 * w + 8, 128], [1, 8 * w]]):
                           e.dma_start(out=_o, in_=_i),
                           (gen_free[(qt, genidx - 1)], gen_free[(qt, genidx - 2)],
                            gen_ev[(qt, genidx - 1)], gen_ev[(qt, genidx - 2)],
                            brd_tok.get((qt - 2, rwidx))), ("SQ", 16),
                           batch=("rw", qt))
                    rw_tok[(qt, rwidx)] = rw
                    rwidx += 1
        last_gen[qt] = gen_mm[(qt, NGEN4 - 1)]
        red = ew_red.get(qt - 1)
        mul = None
        for sl, dlt in EWS:
            tile = qt + dlt
            mul = S("vector", lambda e, _k=ktr[:, (tile % RING) * 1024:((tile % RING) + 1) * 1024]:
                    e.tensor_tensor(scr[:], qtb[:], _k, ALU.mult),
                    (qtb_ev[qt], ring_ld[tile], red), ("VS", 1))
            red = S("vector", lambda e, _o=bandx[:, (sl - 41) * 16:(sl - 40) * 16]:
                    e.tensor_reduce(_o, scr[:].rearrange("p (hh d) -> p hh d", hh=16),
                                    mybir.AxisListType.X, ALU.add), (mul,), ("VS", 1))
        ew_lastmul[qt] = mul
        ew_red[qt] = red

    def stage_B1(qt):
        slot = qt % 2
        brd = []
        rwidx = 0
        for gi, (w0o, w, sb_, ns, ds) in enumerate(GROUPS):
            for half in range(2):
                roff = ((qt % 2) * 2 + half) * 128 * (8 * w + 8)
                if gi == 0:
                    src = bass.AP(t[f"rimg{gi}"], roff, [[8 * w + 16, 128], [1, 33 * 8]])
                    dst = bass.AP(band[slot][:].tensor, half * 8, [[CW, 128], [16, 33], [1, 8]])
                else:
                    src = bass.AP(t[f"rimg{gi}"], roff, [[8 * w + 16, 128], [8 * ds, ns], [1, 8]])
                    dst = bass.AP(band[slot][:].tensor, sb_ * 16 + half * 8, [[CW, 128], [16, ns], [1, 8]])
                brd.append(S("scalar", lambda e, _o=dst, _i=src: e.dma_start(out=_o, in_=_i),
                             (rw_tok[(qt, rwidx)], band_used.get(qt - 2)), ("AQ", 16), batch=("brd", qt)))
                brd_tok[(qt, rwidx)] = brd[-1]
                rwidx += 1
        S("vector", lambda e: e.tensor_copy(band[slot][:, 656:736], bandx[:]),
          (band_used.get(qt - 2), ew_red[qt]), ("VS", 1))
        v1 = S("vector", lambda e: e.tensor_tensor(t2b[:], band[slot][:], qsb[slot][:], ALU.mult),
               tuple(brd) + (gen_free[(qt, 0)], gen_free[(qt, 1)], gen_free[(qt, 2)], gen_free[(qt, 3)], P_tok.get(qt - 1)), ("VS", 1))
        qs_used[qt] = v1
        band_used[qt] = v1
        v3 = S("vector", lambda e: e.scalar_tensor_tensor(t2b[:], t2b[:], 0.125, pbmb[slot][:], ALU.mult, ALU.add),
               (v1, pbm_tok[slot], exp_tok.get(qt - 1)), ("VS", 1))
        pbm_used[qt] = v3
        s1 = S("scalar", lambda e: e.activation(t2b[:], t2b[:], AF.Exp), (v3,), ("SS", 1))
        exp_tok[qt] = s1
        v4 = S("vector", lambda e: e.tensor_reduce(Zb[:], t2b[:].rearrange("p (s hh) -> p hh s", hh=16),
                                                   mybir.AxisListType.X, ALU.add), (s1,), ("VS", 1))
        v5 = S("vector", lambda e: e.reciprocal(rb[:], Zb[:]), (v4,), ("VS", 1))
        rbc = bass.AP(rb[:].tensor, 0, [[16, 128], [0, NSLOT], [1, 16]])
        Ev3 = t2b[:].rearrange("p (s hh) -> p s hh", hh=16)
        Pv3 = Pb[slot][:].rearrange("p (s hh) -> p s hh", hh=16)
        v6 = S("vector", lambda e, _r=rbc, _e=Ev3, _p=Pv3: e.tensor_tensor(_p, _e, _r, ALU.mult),
               (v5, psp_wr.get(qt - 2), pve_cmb.get(qt - 2)), ("VS", 1))
        P_tok[qt] = v6
        ibase = (qt % 2) * NB * 128 * 2048
        wr = []
        wr.append(S("sync", lambda e, _i=Pb[slot][:].rearrange("p (j hh) -> p j hh", hh=16)[:, 0:33, :],
                    _o=bass.AP(t["psp_img"], ibase + LOC_ROW0 * 2048, [[2064, 128], [2048, 33], [1, 16]]):
                    e.dma_start(out=_o, in_=_i),
                    (v6, psp_rd.get((qt - 2, 0)), psp_rd.get((qt - 2, 1))), ("SQ", 16)))
        for sl in range(33, 41):
            if not SLOT_REAL[sl]:
                continue
            o = SLOT_OFF[sl]
            wr.append(S("sync", lambda e, _i=Pb[slot][:, sl * 16:(sl + 1) * 16],
                        _o=bass.AP(t["psp_img"], ibase + DY_ROW0[o] * 2048, [[2064, 128], [1, 16]]):
                        e.dma_start(out=_o, in_=_i), (v6,), ("SQ", 16), batch=("pw", qt)))
        psp_wr[qt] = wr[-1]
        tile = qt + RING
        if tile < NKT:
            ring_ld[tile] = S("gpsimd", lambda e,
                              _o=ktr[:, (tile % RING) * 1024:((tile % RING) + 1) * 1024],
                              _i=bass.AP(t["ktok"], tile * 128 * 1024, [[1024, 128], [1, 1024]]):
                              e.dma_start(out=_o, in_=_i), (ew_lastmul[qt],), ("GQ", 16), batch=("kr", qt))
        if DEBUG:
            S("sync", lambda e, _o=bass.AP(t["dbg_band"], qt * 128 * CW, [[CW, 128], [1, CW]]),
              _i=band[slot][:]: e.dma_start(out=_o, in_=_i), (v2,), ("SQ", 16))
            S("sync", lambda e, _o=bass.AP(t["dbg_qs"], qt * 128 * CW, [[CW, 128], [1, CW]]),
              _i=qsb[slot][:]: e.dma_start(out=_o, in_=_i), (v1,), ("SQ", 16))
            S("sync", lambda e, _o=bass.AP(t["dbg_lg"], qt * 128 * CW, [[CW, 128], [1, CW]]),
              _i=t2b[:]: e.dma_start(out=_o, in_=_i), (exp_tok[qt],), ("SQ", 16))
            S("sync", lambda e, _o=bass.AP(t["dbg_P"], qt * 128 * CW, [[CW, 128], [1, CW]]),
              _i=Pb[slot][:]: e.dma_start(out=_o, in_=_i), (P_tok[qt],), ("SQ", 16))
            if qt == 4:
                S("sync", lambda e, _o=bass.AP(t["dbg_psp"], 0, [[2048, NB * 128], [1, 2048]]),
                  _i=bass.AP(t["psp_img"], (qt % 2) * NB * 128 * 2048, [[2048, NB * 128], [1, 2048]]):
                  e.dma_start(out=_o, in_=_i), (psp_rd.get(qt, wr[-1]),), ("SQ", 16))
    def stage_B2(qt):
        ibase = (qt % 2) * NB * 128 * 2048
        rd = None
        for ib in range(NB):
            src = bass.AP(t["psp_img"], ibase + ib * 128 * 2048, [[2048, 128], [1, 2048]])
            dst = pspsb[:, ib * 2048:(ib + 1) * 2048]
            rd = S("gpsimd", lambda e, _o=dst, _i=src: e.dma_start(out=_o, in_=_i),
                   (psp_wr[qt], pv_tok.get((qt - 1, 7))), ("GQ", 16), batch=("pr", qt))
        psp_rd[qt] = rd
        slot = qt % 2
        m = None
        vlast = None
        for i, (sl, dlt) in enumerate(EWS):
            tile = qt + dlt
            pbc = bass.AP(Pb[slot][:].tensor, sl * 16, [[CW, 128], [1, 16], [0, 64]])
            vsl = vtok[:, tile * 1024:(tile + 1) * 1024]
            if i < 3:
                if i == 0:
                    m = S("gpsimd", lambda e, _v=vsl, _p=pbc:
                          e.tensor_tensor(acch[:], _v, _p, ALU.mult),
                          (P_tok[qt], accT_tok.get(qt - 1)), ("GS", 1))
                else:
                    m = S("gpsimd", lambda e, _v=vsl, _p=pbc:
                          e.tensor_tensor(ptmp[:], _v, _p, ALU.mult), (), ("GS", 1))
                    m = S("gpsimd", lambda e: e.tensor_tensor(acch[:], acch[:], ptmp[:], ALU.add),
                          (), ("GS", 1))
            else:
                if i == 3:
                    vlast = S("vector", lambda e, _v=vsl, _p=pbc:
                              e.tensor_tensor(pvv[:], _v, _p, ALU.mult),
                              (P_tok[qt], accT_tok.get(qt - 1)), ("VS", 1))
                else:
                    vlast = S("vector", lambda e, _v=vsl, _p=pbc:
                              e.tensor_tensor(ptv[:], _v, _p, ALU.mult), (), ("VS", 1))
                    vlast = S("vector", lambda e: e.tensor_tensor(pvv[:], pvv[:], ptv[:], ALU.add),
                              (), ("VS", 1))
        pve_last[qt] = vlast
        pve_cmb[qt] = None
        cst = S("gpsimd", lambda e: e.tensor_tensor(acch[:], acch[:], pvv[:], ALU.add),
                (m, vlast), ("GS", 1))
        pve_cmb[qt] = cst
        accT_tok[qt] = S("sync", lambda e: e.dma_start(
            out=accTb[:].rearrange("p (m t) -> p m t", m=8), in_=acch[:], transpose=True),
            (cst, merge_tok.get(qt - 2)), ("SQ", 16), batch=("pvt", qt))

    for it in range(NQT + 2):
        if 1 <= it <= NQT:
            stage_B1(it - 1)
        if it < NQT:
            stage_A(it)
        if it >= 2 and it >= NQT:
            for hp in range(8):
                stage_C_hp(it - 2, hp)
        if 1 <= it <= NQT:
            stage_B2(it - 1)
    sched.barrier()
    sched.barrier()
    es2.close()
    es1.close()

    if DEBUG:
        S("sync", lambda e: e.dma_start(out=t["dbg_of"].ap(), in_=outfl[:]), (), ("SQ", 16))

    with contextlib.ExitStack() as e2b:
        wo_sb = sb("wo_sb", [128, 8 * 1024], BF16, e2b)
        gated = sb("gated", [128, 8 * TOK], BF16, e2b)
        gsl = [sb(f"gsl{i}", [128, TOK], BF16, e2b) for i in range(2)]
        apj = sb("apj", [128, 8 * TOK], BF16, e2b)
        atile = [sb(f"atile{i}", [128, 1024], BF16, e2b) for i in range(2)]
        xres = [sb(f"xres{i}", [128, 1024], F32, e2b) for i in range(2)]
        x1t = [sb(f"x1t{i}", [128, 1024], F32, e2b) for i in range(2)]
        lnt = [sb(f"lnt{i}", [128, 1024], F32, e2b) for i in range(2)]
        sumc2 = sb("sumc2", [128, 1], F32, e2b)
        negm2 = sb("negm2", [128, 1], F32, e2b)
        ssq2 = sb("ssq2", [128, 1], F32, e2b)
        rstd2 = sb("rstd2", [128, 1], F32, e2b)
        nmr2 = sb("nmr2", [128, 1], F32, e2b)
        sqs2 = sb("sqs2", [128, 1024], F32, e2b)
        opps = [psum(f"opps{i}", [128, 1024], F32, e2b) for i in range(2)]
        trp2 = [psum(f"trp2{i}", [128, 1024], BF16, e2b) for i in range(2)]

        wol = []
        for c in range(8):
            src = bass.AP(t["woT"], (c * 128) * 1024, [[1024, 128], [1, 1024]])
            wol.append(S("gpsimd", lambda e, _o=wo_sb[:, c * 1024:(c + 1) * 1024], _i=src:
                         e.dma_start(out=_o, in_=_i), (), ("GQ", 16)))
        gts = []
        gl_tok = [None, None]
        for c in range(8):
            gslt = c % 2
            gl = S("scalar", lambda e, _o=gsl[gslt][:],
                   _i=bass.AP(t["gate_img"], c * 128 * TOK, [[TOK, 128], [1, TOK]]):
                   e.dma_start(out=_o, in_=_i), (gl_tok[gslt],), ("AQ", 16))
            gt = S("vector", lambda e, _o=gated[:, c * TOK:(c + 1) * TOK],
                   _i0=outfl[:, c * TOK:(c + 1) * TOK], _i1=gsl[gslt][:]:
                   e.tensor_tensor(_o, _i0, _i1, ALU.mult), (gl,), ("VS", 1))
            gl_tok[gslt] = gt
            gts.append(gt)
        opev = {}
        for dc in range(8):
            for tb in range(2):
                pslot = (dc * 2 + tb) % 2
                mm = None
                for c in range(8):
                    for hn in range(2):
                        mov = gated[:, c * TOK + tb * 1024 + hn * 512: c * TOK + tb * 1024 + (hn + 1) * 512]
                        mm = S("tensor", lambda e, _o=opps[pslot][:, hn * 512:(hn + 1) * 512],
                               _s=wo_sb[:, c * 1024 + dc * 128: c * 1024 + (dc + 1) * 128], _m=mov,
                               _st=(c == 0), _sp=(c == 7):
                               e.matmul(_o, _s, _m, start=_st, stop=_sp),
                               (gts[c], wol[c], opev.get((dc, tb - 2)) or opev.get((dc - 1, tb))), ("TS", 1))
                ev = S("scalar", lambda e, _o=apj[:, dc * TOK + tb * 1024: dc * TOK + (tb + 1) * 1024],
                       _i=opps[pslot][:], _b=obcol[:, dc:dc + 1]:
                       e.activation(_o, _i, AF.Identity, bias=_b), (mm,), ("SS", 1))
                opev[(dc, tb)] = ev
        prev_sp = [None, None]
        prev_ev3 = [None, None]
        for tt in range(NQT):
            bs = tt % 2
            xr = S("scalar", lambda e, _o=xres[bs][:],
                   _i=bass.AP(t["xs"], (HALO + tt * 128) * D, [[D, 128], [1, D]]):
                   e.dma_start(out=_o, in_=_i), (prev_sp[bs],), ("AQ", 16))
            tps = None
            for dc in range(8):
                tps = S("tensor", lambda e, _o=trp2[bs][:, dc * 128:(dc + 1) * 128],
                        _i=apj[:, dc * TOK + tt * 128: dc * TOK + (tt + 1) * 128]:
                        e.transpose(_o, _i, idb[:]),
                        (opev[(dc, tt // 8)], prev_ev3[bs]), ("TS", 1))
            ev = S("vector", lambda e, _o=atile[bs][:], _i=trp2[bs][:]: e.tensor_copy(_o, _i), (tps,), ("VS", 1))
            prev_ev3[bs] = ev
            v1 = S("vector", lambda e, _x=atile[bs][:]: e.tensor_reduce(sumc2[:], _x, mybir.AxisListType.X, ALU.add), (ev,), ("VS", 1))
            v2 = S("vector", lambda e: e.tensor_scalar(negm2[:], sumc2[:], -1.0 / D, None, ALU.mult), (v1,), ("VS", 1))
            s1 = S("scalar", lambda e, _x=atile[bs][:]: e.activation(sqs2[:], _x, AF.Square, bias=negm2[:], scale=1.0, accum_out=ssq2[:]), (v2,), ("SS", 1))
            s2a = S("scalar", lambda e: e.activation(rstd2[:], ssq2[:], AF.Sqrt, bias=epsc[:], scale=1.0 / D), (s1,), ("SS", 1))
            s2 = S("vector", lambda e: e.reciprocal(rstd2[:], rstd2[:]), (s2a,), ("VS", 1))
            v3 = S("vector", lambda e: e.tensor_mul(nmr2[:], negm2[:], rstd2[:]), (s2,), ("VS", 1))
            s3 = S("scalar", lambda e, _x=atile[bs][:], _y=lnt[bs][:]:
                   e.activation(_y, _x, AF.Identity, bias=nmr2[:], scale=rstd2[:]), (v3,), ("SS", 1))
            v7 = S("vector", lambda e, _o=x1t[bs][:], _a=xres[bs][:], _b=lnt[bs][:]:
                   e.tensor_tensor(_o, _a, _b, ALU.add), (s3, xr), ("VS", 1))
            sp = S("sync", lambda e, _i=x1t[bs][:],
                   _o=bass.AP(t["x1_img"], tt * 128 * D, [[D, 128], [1, D]]):
                   e.dma_start(out=_o, in_=_i), (v7,), ("SQ", 16))
            prev_sp[bs] = sp
            if DEBUG:
                S("sync", lambda e, _i=x1t[bs][:],
                  _o=bass.AP(t["dbg_x1"], tt * 128 * D, [[D, 128], [1, D]]):
                  e.dma_start(out=_o, in_=_i), (v7,), ("SQ", 16))
        sched.barrier()

    esOF.close()

    with contextlib.ExitStack() as e3:
        wbig = sb("wbig", [128, 8 * 1024], BF16, e3)
        wscr = sb("wscr", [128, 8 * 512], F32, e3)
        g1rep = [sb(f"g1rep{i}", [128, 32], F32, e3) for i in range(2)]
        g2rep = [sb(f"g2rep{i}", [128, 16], F32, e3) for i in range(2)]
        tbuf = sb("tbuf", [128, 512], F32, e3)
        x2fm = sb("x2fm", [128, 8 * TOK], BF16, e3)
        ffm = sb("ffm", [128, 8 * TOK], BF16, e3)
        x1r = [sb(f"x1r{i}", [128, 1024], F32, e3) for i in range(2)]
        x2t = [sb(f"x2t{i}", [128, 1024], BF16, e3) for i in range(2)]
        ftile = [sb(f"ftile{i}", [128, 1024], BF16, e3) for i in range(2)]
        lnt3 = [sb(f"lnt3{i}", [128, 1024], F32, e3) for i in range(2)]
        otile = [sb(f"otile{i}", [128, 1024], F32, e3) for i in range(2)]
        sumc3 = sb("sumc3", [128, 1], F32, e3)
        negm3 = sb("negm3", [128, 1], F32, e3)
        ssq3 = sb("ssq3", [128, 1], F32, e3)
        rstd3 = sb("rstd3", [128, 1], F32, e3)
        nmr3 = sb("nmr3", [128, 1], F32, e3)
        sqs3 = sb("sqs3", [128, 1024], F32, e3)
        fps = [psum(f"fps{i}", [128, 1024], F32, e3) for i in range(2)]
        trp3 = [psum(f"trp3{i}", [128, 1024], BF16, e3) for i in range(2)]

        x2v = x2fm[:].rearrange("p (c tq) -> p c tq", c=8)
        wb_tok = {}
        gtoks = [None, None]
        for ri, (g1n, g2n) in enumerate((("g1rh", "g2rh"), ("g1ih", "g2ih"))):
            for ic in range(4):
                acc_tok = None
                for c in range(4):
                    gslt = c % 2
                    l1 = S("gpsimd", lambda e, _o=g1rep[gslt][:],
                           _i=bass.AP(t[g1n], (c * 4 + ic) * 128 * 32, [[32, 128], [1, 32]]):
                           e.dma_start(out=_o, in_=_i), (gtoks[gslt],), ("GQ", 16))
                    l2 = S("gpsimd", lambda e, _o=g2rep[gslt][:],
                           _i=bass.AP(t[g2n], c * 128 * 16, [[16, 128], [1, 16]]):
                           e.dma_start(out=_o, in_=_i), (gtoks[gslt],), ("GQ", 16))
                    g1ap = bass.AP(g1rep[gslt][:].tensor, 0, [[32, 128], [1, 32], [0, 16]])
                    g2ap = bass.AP(g2rep[gslt][:].tensor, 0, [[16, 128], [0, 32], [1, 16]])
                    dst = wscr[:, (ri * 4 + ic) * 512:(ri * 4 + ic + 1) * 512] if c == 0 else tbuf[:]
                    mul = S("vector", lambda e, _o=dst, _a=g1ap, _b=g2ap:
                            e.tensor_tensor(_o, _a, _b, ALU.mult), (l1, l2), ("VS", 1))
                    gtoks[gslt] = mul
                    if c > 0:
                        wsl = wscr[:, (ri * 4 + ic) * 512:(ri * 4 + ic + 1) * 512]
                        acc_tok = S("vector", lambda e, _o=wsl, _b=tbuf[:]:
                                    e.tensor_tensor(_o, _o, _b, ALU.add), (mul, acc_tok), ("VS", 1))
                    else:
                        acc_tok = mul
                wb_tok[(ri, ic)] = acc_tok
        wq_tok = []
        for icg in range(8):
            half_in = icg // 4
            ic = icg % 4
            for oh in range(2):
                if half_in == 0:
                    srcw = wscr[:, (oh * 4 + ic) * 512:(oh * 4 + ic + 1) * 512]
                    sign = 1.0
                    wwait = wb_tok[(oh, ic)]
                else:
                    srcw = wscr[:, ((1 - oh) * 4 + ic) * 512:((1 - oh) * 4 + ic + 1) * 512]
                    sign = -1.0 if oh == 0 else 1.0
                    wwait = wb_tok[(1 - oh, ic)]
                wq_tok.append(S("vector", lambda e, _o=wbig[:, icg * 1024 + oh * 512: icg * 1024 + (oh + 1) * 512],
                                _i=srcw, _s=sign:
                                e.tensor_scalar(_o, _i, _s, None, ALU.mult), (wwait,), ("VS", 1)))
        wq_last = wq_tok[-1]

        prev_ap = [None, None]
        x2_tok = {}
        for tt in range(NQT):
            bs = tt % 2
            ld = S("scalar", lambda e, _o=x1r[bs][:],
                   _i=bass.AP(t["x1_img"], tt * 128 * D, [[D, 128], [1, D]]):
                   e.dma_start(out=_o, in_=_i), (prev_ap[bs],), ("AQ", 16))
            v1 = S("vector", lambda e, _x=x1r[bs][:]: e.tensor_reduce(sumc3[:], _x, mybir.AxisListType.X, ALU.add), (ld,), ("VS", 1))
            v2 = S("vector", lambda e: e.tensor_scalar(negm3[:], sumc3[:], -1.0 / D, None, ALU.mult), (v1,), ("VS", 1))
            s1 = S("scalar", lambda e, _x=x1r[bs][:]: e.activation(sqs3[:], _x, AF.Square, bias=negm3[:], scale=1.0, accum_out=ssq3[:]), (v2,), ("SS", 1))
            s2a = S("scalar", lambda e: e.activation(rstd3[:], ssq3[:], AF.Sqrt, bias=epsc[:], scale=1.0 / D), (s1,), ("SS", 1))
            s2 = S("vector", lambda e: e.reciprocal(rstd3[:], rstd3[:]), (s2a,), ("VS", 1))
            v3 = S("vector", lambda e: e.tensor_mul(nmr3[:], negm3[:], rstd3[:]), (s2,), ("VS", 1))
            s3 = S("scalar", lambda e, _x=x1r[bs][:], _y=x2t[bs][:]:
                   e.activation(_y, _x, AF.Identity, bias=nmr3[:], scale=rstd3[:]), (v3,), ("SS", 1))
            prev_ap[bs] = s3
            tps = None
            for c in range(8):
                tps = S("tensor", lambda e, _o=trp3[bs][:, c * 128:(c + 1) * 128],
                        _i=x2t[bs][:, c * 128:(c + 1) * 128]:
                        e.transpose(_o, _i, idb[:]), (s3, x2_tok.get(tt - 2)), ("TS", 1))
            ev = S("vector", lambda e, _o=x2v[:, :, tt * 128:(tt + 1) * 128],
                   _i=trp3[bs][:].rearrange("p (c tq) -> p c tq", c=8):
                   e.tensor_copy(_o, _i), (tps,), ("VS", 1))
            x2_tok[tt] = ev

        fev = {}
        for oc in range(8):
            for tb in range(2):
                pslot = (oc * 2 + tb) % 2
                mm = None
                for ic in range(8):
                    for hn in range(2):
                        mov = x2v[:, ic, tb * 1024 + hn * 512: tb * 1024 + (hn + 1) * 512]
                        mm = S("tensor", lambda e, _o=fps[pslot][:, hn * 512:(hn + 1) * 512],
                               _s=wbig[:, ic * 1024 + oc * 128: ic * 1024 + (oc + 1) * 128], _m=mov,
                               _st=(ic == 0), _sp=(ic == 7):
                               e.matmul(_o, _s, _m, start=_st, stop=_sp),
                               (wq_last, x2_tok[tb * 8 + 7], fev.get((oc, tb - 2)) or fev.get((oc - 1, tb))), ("TS", 1))
                ev = S("scalar", lambda e, _o=ffm[:, oc * TOK + tb * 1024: oc * TOK + (tb + 1) * 1024],
                       _i=fps[pslot][:], _b=fbcol[:, oc:oc + 1]:
                       e.activation(_o, _i, AF.Identity, bias=_b), (mm,), ("SS", 1))
                fev[(oc, tb)] = ev

        prev_ot = [None, None]
        prev_ev4 = [x2_tok.get(NQT - 2), x2_tok.get(NQT - 1)]
        out_toks = []
        for tt in range(NQT):
            bs = tt % 2
            ld = S("scalar", lambda e, _o=x1r[bs][:],
                   _i=bass.AP(t["x1_img"], tt * 128 * D, [[D, 128], [1, D]]):
                   e.dma_start(out=_o, in_=_i), (prev_ot[bs], prev_ap[bs]), ("AQ", 16))
            tps = None
            for oc in range(8):
                tps = S("tensor", lambda e, _o=trp3[bs][:, oc * 128:(oc + 1) * 128],
                        _i=ffm[:, oc * TOK + tt * 128: oc * TOK + (tt + 1) * 128]:
                        e.transpose(_o, _i, idb[:]), (fev[(oc, tt // 8)], prev_ev4[bs]), ("TS", 1))
            ev = S("vector", lambda e, _o=ftile[bs][:], _i=trp3[bs][:]: e.tensor_copy(_o, _i), (tps,), ("VS", 1))
            prev_ev4[bs] = ev
            v1 = S("vector", lambda e, _x=ftile[bs][:]: e.tensor_reduce(sumc3[:], _x, mybir.AxisListType.X, ALU.add), (ev,), ("VS", 1))
            v2 = S("vector", lambda e: e.tensor_scalar(negm3[:], sumc3[:], -1.0 / D, None, ALU.mult), (v1,), ("VS", 1))
            s1 = S("scalar", lambda e, _x=ftile[bs][:]: e.activation(sqs3[:], _x, AF.Square, bias=negm3[:], scale=1.0, accum_out=ssq3[:]), (v2,), ("SS", 1))
            s2a = S("scalar", lambda e: e.activation(rstd3[:], ssq3[:], AF.Sqrt, bias=epsc[:], scale=1.0 / D), (s1,), ("SS", 1))
            s2 = S("vector", lambda e: e.reciprocal(rstd3[:], rstd3[:]), (s2a,), ("VS", 1))
            v3 = S("vector", lambda e: e.tensor_mul(nmr3[:], negm3[:], rstd3[:]), (s2,), ("VS", 1))
            s3 = S("scalar", lambda e, _x=ftile[bs][:], _y=lnt3[bs][:]:
                   e.activation(_y, _x, AF.Identity, bias=nmr3[:], scale=rstd3[:]), (v3,), ("SS", 1))
            v7 = S("vector", lambda e, _o=otile[bs][:], _a=x1r[bs][:], _b=lnt3[bs][:]:
                   e.tensor_tensor(_o, _a, _b, ALU.add), (s3, ld), ("VS", 1))
            ot = S("sync", lambda e, _i=otile[bs][:],
                   _o=bass.AP(t["out"], tt * 128 * D, [[D, 128], [1, D]]):
                   e.dma_start(out=_o, in_=_i), (v7,), ("SQ", 16))
            prev_ot[bs] = ot
            out_toks.append(ot)
        S("sync", None, (out_toks[-1],), None)

    with nc.allow_non_contiguous_dma(reason="tiny strided constant loads"):
        emit(nc, sched, sems)
    es_all.close()
    return nc


def _expand_ifgcol(ifg):
    p = np.arange(128)
    hp = np.arange(8)
    return np.ascontiguousarray(ifg[(2 * hp[None, :] + p[:, None] // 64)]).astype(np.float32)


def _expand_g1(g1):
    g1 = g1.reshape(32, 4, 32)
    out = np.empty((4, 4, 128, 32), np.float32)
    p = np.arange(128)
    for c in range(4):
        for ic in range(4):
            d = ic * 8 + p // 16
            out[c, ic] = g1[:, c, d].T
    return np.ascontiguousarray(out)


def _expand_g2(g2):
    g2 = g2.reshape(4, 16, 16)
    out = np.empty((4, 128, 16), np.float32)
    p = np.arange(128)
    for c in range(4):
        out[c] = g2[c, :, p % 16]
    return np.ascontiguousarray(out)


def _build_inputs(ins):
    x = np.ascontiguousarray(np.asarray(ins["L0_attn_x"], np.float32))
    qkv_w = np.asarray(ins["L0_attn_qkv_w"], np.float32)
    gate_w = np.asarray(ins["L0_attn_gate_w"], np.float32)
    out_w = np.asarray(ins["L0_attn_out_w"], np.float32)
    se = np.asarray(ins["L0_attn_scale_embed"], np.float32)
    pb = np.asarray(ins["L0_attn_pos_bias"], np.float32)

    se_slots = np.zeros((64, NSLOT), np.float32)
    for sl in range(NSLOT):
        o = SLOT_OFF[sl] if SLOT_REAL[sl] else 48
        se_slots[:, sl] = se[OIDX[o]]
    common = dict(
        wqkvT=np.ascontiguousarray(qkv_w.T),
        wgT=np.ascontiguousarray(gate_w.T),
        woT=np.ascontiguousarray(out_w.T),
        qkvb=np.asarray(ins["L0_attn_qkv_b"], np.float32),
        gateb=np.asarray(ins["L0_attn_gate_b"], np.float32),
        outb=np.asarray(ins["L0_attn_out_b"], np.float32),
        ffnb=np.concatenate([np.asarray(ins["L0_ffn_bias_real"], np.float32),
                             np.asarray(ins["L0_ffn_bias_imag"], np.float32)]),
        se2=np.concatenate([se_slots, se_slots], axis=0),
        ifg=np.asarray(ins["L0_attn_if_gain"], np.float32),
        ifgcolh=_expand_ifgcol(np.asarray(ins["L0_attn_if_gain"], np.float32)),
        g1rh=_expand_g1(np.asarray(ins["L0_ffn_G1_R"], np.float32)),
        g2rh=_expand_g2(np.asarray(ins["L0_ffn_G2_R"], np.float32)),
        g1ih=_expand_g1(np.asarray(ins["L0_ffn_G1_I"], np.float32)),
        g2ih=_expand_g2(np.asarray(ins["L0_ffn_G2_I"], np.float32)),
        ident=np.eye(128, dtype=np.float32),
    )
    in_maps = []
    for core in range(8):
        b, qq = core // 4, core % 4
        s0 = qq * TOK
        xs = np.zeros((TOT, D), np.float32)
        src = x[b, max(s0 - HALO, 0): s0 + TOK]
        xs[TOT - src.shape[0]:] = src
        pbm = np.full((NQT, 128, NSLOT, 16), -50.0, np.float32)
        gpos = s0 + np.arange(TOK).reshape(NQT, 128)
        for sl in range(NSLOT):
            if not SLOT_REAL[sl]:
                continue
            o = SLOT_OFF[sl]
            m = gpos >= o
            pbm[:, :, sl, :] = np.where(m[:, :, None], pb[OIDX[o]][None, None, :], -50.0)
        d = dict(common)
        d["xs"] = xs
        d["pbm"] = np.ascontiguousarray(pbm.reshape(NQT, 128, CW)).astype(np.float16)
        in_maps.append(d)
    return in_maps


_prog_cache = {}


def kernel(**inputs):
    in_maps = _build_inputs(inputs)
    if "nc" not in _prog_cache:
        _prog_cache["nc"] = build_program()
    nc = _prog_cache["nc"]
    res = run_bass_kernel_spmd(nc, in_maps, core_ids=list(range(8)))
    out = np.empty((B, N, D), np.float32)
    for core in range(8):
        b, qq = core // 4, core % 4
        out[b, qq * TOK:(qq + 1) * TOK] = res.results[core]["out"]
    return out


# version 23
# speedup vs baseline: 1.0558x; 1.0558x over previous
import contextlib
from collections import defaultdict

import numpy as np

import concourse.bass as bass
import concourse.mybir as mybir
from concourse.bass_utils import run_bass_kernel_spmd

F32 = mybir.dt.float32
BF16 = mybir.dt.float16
AF = mybir.ActivationFunctionType
ALU = mybir.AluOpType

DEBUG = False
SERIAL2 = False

B, N, D, H, HD = 2, 8192, 1024, 16, 64
TOK, HALO, TOT = 2048, 1536, 3584
NT = TOT // 128
NQT = TOK // 128
NBLK = 7
NSLOT = 46
CW = NSLOT * 16

DENSE = list(range(33))
DYAD = [48, 64, 96, 128, 192, 256, 384, 512, 768, 1024, 1536]
OFFSETS = DENSE + DYAD
OIDX = {o: i for i, o in enumerate(OFFSETS)}
SLOT_OFF = [32 - s for s in range(33)] + [128, 112, 96, 80, 64, 48] + \
           [256, 192] + [512, 384] + [768] + [1024] + [1536]
SLOT_REAL = [True] * 33 + [True, False, True, False, True, True] + [True] * 7

GROUPS = [
    (-32, 160, 0, 33, 1),
    (-128, 208, 33, 6, 16),
    (-256, 192, 39, 2, 64),
]
EWS = [(41, 8), (42, 9), (43, 6), (44, 4), (45, 0)]
RING = 10
NKT = 25
KB0 = 1280
KW = TOT - KB0
NG = len(GROUPS)
NGEN = NG * 2
NGEN4 = NG * 4
BO = [-2, -1, 0]
NB = len(BO)
DY_ROW0 = {}
for o in DYAD:
    bo_base = -((o + 127) // 128)
    if bo_base in BO:
        DY_ROW0[o] = BO.index(bo_base) * 128 - o - bo_base * 128
LOC_ROW0 = BO.index(-1) * 128 - 32 + 128

ENGINES = ("sync", "tensor", "vector", "scalar", "gpsimd")
DMASEMS = ("SQ", "AQ", "GQ")


class Sched:
    def __init__(self):
        self.q = {e: [] for e in ENGINES}
        self.cnt = defaultdict(int)
        self.last = {e: None for e in ENGINES}
        self.batch = {}
        self.nbatch = 0

    def op(self, eng, fn, waits=(), inc=None, batch=None):
        tok = None
        pre = None
        if inc is not None:
            sem, amt = inc
            if sem in DMASEMS:
                if batch is None:
                    self.nbatch += 1
                    batch = ("_auto", self.nbatch)
                if self.batch.get(sem) != batch:
                    pre = (sem, self.cnt[sem])
                    self.batch[sem] = batch
            self.cnt[sem] += amt
            tok = (sem, self.cnt[sem])
        ws = [(sm, self.cnt[sm] if sm in DMASEMS else v)
              for (sm, v) in waits if (sm, v) is not None and v is not None
              ] if False else []
        for w in waits:
            if w is None:
                continue
            sm, v = w
            ws.append((sm, self.cnt[sm] - (16 if w is tok else 0)) if sm in DMASEMS else (sm, v))
        if pre is not None:
            ws.append(pre)
        is_dma = inc is not None and inc[0] in DMASEMS
        if eng != "tensor" and not is_dma and self.last[eng] is not None:
            ws.append(self.last[eng])
        if tok is not None and not is_dma:
            self.last[eng] = tok
        self.q[eng].append((fn, ws, tok))
        return tok

    def barrier(self):
        now = [(s, self.cnt[s]) for s in ("TS", "VS", "SS", "GS") + DMASEMS if self.cnt[s] > 0]
        toks = []
        for e in ENGINES:
            toks.append(self.op(e, None, now, None))
        return toks


def emit(nc, sched, sems):
    with nc.Block() as block:
        for eng in ENGINES:
            ops = sched.q[eng]

            def make_body(ops):
                def body(e):
                    seen = {}
                    for fn, waits, tok in ops:
                        for (s, v) in waits:
                            if seen.get(s, -1) < v:
                                e.wait_ge(sems[s], v)
                                seen[s] = v
                        if fn is None:
                            if tok is not None:
                                e.nop(nofuse=True).then_inc(sems[tok[0]], 1)
                            continue
                        instr = fn(e)
                        if tok is not None:
                            sem, _ = tok
                            instr.then_inc(sems[sem], 16 if sem in DMASEMS else 1)
                return body
            getattr(block, eng)(make_body(ops))


def build_program():
    nc = bass.Bass("TRN2", target_bir_lowering=False)
    t = {}

    def din(name, shape, dtype=F32):
        t[name] = nc.dram_tensor(name, shape, dtype, kind="ExternalInput")

    def dout(name, shape, dtype=F32):
        t[name] = nc.dram_tensor(name, shape, dtype, kind="ExternalOutput")

    def dint(name, shape, dtype=BF16):
        t[name] = nc.dram_tensor(name, shape, dtype, kind="Internal")

    din("xs", [TOT, D]); din("wqkvT", [D, 3 * D]); din("wgT", [D, D]); din("woT", [D, D])
    din("qkvb", [3 * D]); din("gateb", [D]); din("outb", [D]); din("ffnb", [D])
    din("se2", [128, NSLOT]); din("pbm", [NQT, 128, CW], BF16); din("ifg", [16])
    din("ifgcolh", [128, 8])
    din("g1rh", [4, 4, 128, 32]); din("g2rh", [4, 128, 16])
    din("g1ih", [4, 4, 128, 32]); din("g2ih", [4, 128, 16])
    din("ident", [128, 128])
    dout("out", [TOK, D])
    if DEBUG:
        dout("dbg_kfm", [128, 8 * TOT], BF16); dout("dbg_vt", [128, NT * 1024], BF16)
        dout("dbg_y1", [128, 8 * TOT], BF16); dout("dbg_of", [128, 8 * TOK], BF16)
        dout("dbg_band", [NQT, 128, CW], BF16); dout("dbg_qs", [NQT, 128, CW], BF16)
        dout("dbg_P", [NQT, 128, CW], BF16); dout("dbg_lg", [NQT, 128, CW])
        dout("dbg_x1", [TOK, D]); dout("dbg_psp", [NB * 128, 2048], BF16)

    dint("q_img", [8 * 128 * TOK]); dint("gate_img", [8 * 128 * TOK])
    dint("x1_img", [TOK, D], F32)
    for gi, (w0o, w, sb_, ns, ds) in enumerate(GROUPS):
        dint(f"rimg{gi}", [2 * 2 * 128 * (8 * w + 8)])
    dint("psp_img", [2 * NB * 128 * 2048])
    dint("ktok", [NKT * 128 * 1024])
    dint("qtok", [NQT * 128 * 1024])

    sched = Sched()
    S = sched.op

    es_all = contextlib.ExitStack()
    sems = {}
    for sname in DMASEMS + ("TS", "VS", "SS", "GS") + tuple("bar_" + e for e in ENGINES):
        sems[sname] = es_all.enter_context(nc.semaphore(sname))

    def sb(name, shape, dtype, stack):
        return stack.enter_context(nc.sbuf_tensor(name, shape, dtype))

    def psum(name, shape, dtype, stack):
        return stack.enter_context(nc.psum_tensor(name, shape, dtype))

    es0 = es_all
    se2b = sb("se2b", [128, NSLOT], BF16, es0)
    idb = sb("idb", [128, 128], BF16, es0)
    kbcol = sb("kbcol", [128, 8], F32, es0)
    qbcol = sb("qbcol", [128, 8], F32, es0)
    gbcol = sb("gbcol", [128, 8], F32, es0)
    obcol = sb("obcol", [128, 8], F32, es0)
    fbcol = sb("fbcol", [128, 8], F32, es0)
    bvifg = sb("bvifg", [128, 8], F32, es0)
    epsc = sb("epsc", [128, 1], F32, es0)

    with contextlib.ExitStack() as es:
        se2f = sb("se2f", [128, NSLOT], F32, es)
        idf = sb("idf", [128, 128], F32, es)
        bvcol = sb("bvcol", [128, 8], F32, es)
        ifgcol = sb("ifgcol", [128, 8], F32, es)
        zzb = sb("zzb", [128, 2048], BF16, es)

        qb = t["qkvb"]
        tl = S("gpsimd", lambda e: e.dma_start(out=se2f[:], in_=t["se2"].ap()), (), ("GQ", 16))
        ti = S("gpsimd", lambda e: e.dma_start(out=idf[:], in_=t["ident"].ap()), (), ("GQ", 16))
        tb1 = S("gpsimd", lambda e: e.dma_start(out=kbcol[:], in_=bass.AP(qb, 1024, [[1, 128], [128, 8]])), (), ("GQ", 16))
        tb2 = S("gpsimd", lambda e: e.dma_start(out=qbcol[:], in_=bass.AP(qb, 0, [[1, 128], [128, 8]])), (), ("GQ", 16))
        tb3 = S("gpsimd", lambda e: e.dma_start(out=bvcol[:], in_=bass.AP(qb, 2048, [[1, 128], [128, 8]])), (), ("GQ", 16))
        tb4 = S("gpsimd", lambda e: e.dma_start(out=gbcol[:], in_=bass.AP(t["gateb"], 0, [[1, 128], [128, 8]])), (), ("GQ", 16))
        tb5 = S("gpsimd", lambda e: e.dma_start(out=obcol[:], in_=bass.AP(t["outb"], 0, [[1, 128], [128, 8]])), (), ("GQ", 16))
        tb6 = S("gpsimd", lambda e: e.dma_start(out=fbcol[:], in_=bass.AP(t["ffnb"], 0, [[1, 128], [128, 8]])), (), ("GQ", 16))
        tb7 = S("gpsimd", lambda e: e.dma_start(out=ifgcol[:], in_=t["ifgcolh"].ap()), (), ("GQ", 16))

        c1 = S("vector", lambda e: e.tensor_copy(se2b[:], se2f[:]), (tl,), ("VS", 1))
        c2 = S("vector", lambda e: e.tensor_copy(idb[:], idf[:]), (ti,), ("VS", 1))
        c3 = S("vector", lambda e: e.tensor_mul(bvifg[:], bvcol[:], ifgcol[:]), (tb3, tb7), ("VS", 1))
        zz = S("vector", lambda e: e.memset(zzb[:], 0.0), (), ("VS", 1))
        S("vector", lambda e: e.memset(epsc[:], 1e-5), (), ("VS", 1))
        last_pz = None
        for half in range(2):
            for ib in range(NB):
                off = half * NB * 128 * 2048 + ib * 128 * 2048
                last_pz = S("sync", lambda e, _o=off: e.dma_start(
                    out=bass.AP(t["psp_img"], _o, [[2048, 128], [1, 2048]]), in_=zzb[:]), (zz,), ("SQ", 16))
        sched.barrier()

    esOF = contextlib.ExitStack()
    outfl = sb("outfl", [128, 8 * TOK], BF16, esOF)
    es1 = contextlib.ExitStack()
    kfm = sb("kfm", [128, 8 * KW], BF16, es1)
    vtok = sb("vtok", [128, NT * 1024], BF16, es1)
    es1y = contextlib.ExitStack()
    y1fm = sb("y1fm", [128, 8 * TOT], BF16, es1y)
    y1v = y1fm[:].rearrange("p (c tq) -> p c tq", c=8)

    with contextlib.ExitStack() as e1a:
        xt = [sb(f"xt{i}", [128, D], F32, e1a) for i in range(2)]
        y1t = [sb(f"y1t{i}", [128, D], BF16, e1a) for i in range(2)]
        sumc = [sb(f"sumc{i}", [128, 1], F32, e1a) for i in range(2)]
        negm = [sb(f"negm{i}", [128, 1], F32, e1a) for i in range(2)]
        ssq = [sb(f"ssq{i}", [128, 1], F32, e1a) for i in range(2)]
        rstd = [sb(f"rstd{i}", [128, 1], F32, e1a) for i in range(2)]
        nmr = [sb(f"nmr{i}", [128, 1], F32, e1a) for i in range(2)]
        trp = [psum(f"trp{i}", [128, 1024], BF16, e1a) for i in range(2)]

        ln_tok = [None, None]
        ev_tok = [None, None]
        tps_tok = [None, None]
        tps_all = {}
        for tt in range(NT):
            bs = tt % 2
            ld = S("gpsimd", lambda e, _o=xt[bs][:], _i=bass.AP(t["xs"], tt * 128 * D, [[D, 128], [1, D]]):
                   e.dma_start(out=_o, in_=_i), (ln_tok[bs],), ("GQ", 16))
            v1 = S("vector", lambda e, _x=xt[bs][:], _o=sumc[bs][:]: e.tensor_reduce(_o, _x, mybir.AxisListType.X, ALU.add), (ld,), ("VS", 1))
            v2 = S("vector", lambda e, _o=negm[bs][:], _i=sumc[bs][:]: e.tensor_scalar(_o, _i, -1.0 / D, None, ALU.mult), (v1,), ("VS", 1))
            s1 = S("scalar", lambda e, _x=xt[bs][:], _y=y1t[bs][:], _b=negm[bs][:], _a=ssq[bs][:]:
                   e.activation(_y, _x, AF.Square, bias=_b, scale=1.0, accum_out=_a), (v2, tps_tok[bs]), ("SS", 1))
            s2a = S("scalar", lambda e, _o=rstd[bs][:], _i=ssq[bs][:]: e.activation(_o, _i, AF.Sqrt, bias=epsc[:], scale=1.0 / D), (s1,), ("SS", 1))
            s2 = S("vector", lambda e, _r=rstd[bs][:]: e.reciprocal(_r, _r), (s2a,), ("VS", 1))
            v3 = S("vector", lambda e, _o=nmr[bs][:], _m=negm[bs][:], _r=rstd[bs][:]: e.tensor_mul(_o, _m, _r), (s2,), ("VS", 1))
            s3 = S("scalar", lambda e, _x=xt[bs][:], _y=y1t[bs][:], _b=nmr[bs][:], _r=rstd[bs][:]:
                   e.activation(_y, _x, AF.Identity, bias=_b, scale=_r), (v3,), ("SS", 1))
            ln_tok[bs] = s3
            tps = None
            for c in range(8):
                tps = S("tensor", lambda e, _o=trp[bs][:, c * 128:(c + 1) * 128],
                        _i=y1t[bs][:, c * 128:(c + 1) * 128]:
                        e.transpose(_o, _i, idb[:]), (s3, ev_tok[bs], c2), ("TS", 1))
            tps_tok[bs] = tps
            tps_all[tt] = tps
            if tt >= 1:
                pb = (tt - 1) % 2
                ev_tok[pb] = S("vector", lambda e, _o=y1v[:, :, (tt - 1) * 128: tt * 128],
                               _i=trp[pb][:].rearrange("p (c tq) -> p c tq", c=8):
                               e.tensor_copy(_o, _i), (tps_all[tt - 1],), ("VS", 1))
        ev_tok[(NT - 1) % 2] = S("vector", lambda e, _o=y1v[:, :, (NT - 1) * 128: NT * 128],
                                 _i=trp[(NT - 1) % 2][:].rearrange("p (c tq) -> p c tq", c=8):
                                 e.tensor_copy(_o, _i), (tps_all[NT - 1],), ("VS", 1))
        sched.barrier()

    if DEBUG:
        S("sync", lambda e: e.dma_start(out=t["dbg_y1"].ap(), in_=y1fm[:]), (), ("SQ", 16))

    with contextlib.ExitStack() as e1k:
        ifgcol = sb("ifgcol2", [128, 8], F32, e1k)
        wcol = [sb(f"wcol{i}", [128, 8 * 128], BF16, e1k) for i in range(2)]
        qtmp = [sb("qtmp0", [128, 512], BF16, e1k)] * 2
        vtmp = [sb("vtmp0", [128, 512], BF16, e1k)] * 2
        qts = sb("qts", [128, 512], BF16, e1k)
        kearly = sb("kearly", [128, 8 * KB0], BF16, e1k)
        kqps = [psum(f"kqps{i}", [128, 512], F32, e1k) for i in range(2)]
        trpv = [psum(f"trpv{i}", [128, 512], BF16, e1k) for i in range(2)]
        trpk = [psum(f"trpk{i}", [128, 1024], BF16, e1k) for i in range(2)]
        S("gpsimd", lambda e: e.dma_start(out=ifgcol[:], in_=t["ifgcolh"].ap()), (), ("GQ", 16))
        jobs = [(0, oc) for oc in range(8)] + [(3, oc) for oc in range(8)] + \
               [(1, oc) for oc in range(8)] + [(2, oc) for oc in range(8)]
        wl_tok = [None, None]
        prev_job_last_ev = None
        last_k_ev = None
        qtp_tok = {}
        last_qts_ev = None
        last_qts_dma = None
        sp_tok = {}
        vev_tok = {}
        vtp_tok = {}
        for ji, (kind, oc) in enumerate(jobs):
            ws = ji % 2
            if kind == 0:
                wsrc = bass.AP(t["wqkvT"], 1024 + oc * 128, [[3072, 128], [128 * 3072, 8], [1, 128]])
            elif kind == 1:
                wsrc = bass.AP(t["wqkvT"], oc * 128, [[3072, 128], [128 * 3072, 8], [1, 128]])
            elif kind == 3:
                wsrc = bass.AP(t["wqkvT"], 2048 + oc * 128, [[3072, 128], [128 * 3072, 8], [1, 128]])
            else:
                wsrc = bass.AP(t["wgT"], oc * 128, [[1024, 128], [128 * 1024, 8], [1, 128]])
            wl = S("gpsimd", lambda e, _o=wcol[ws][:], _i=wsrc: e.dma_start(out=_o, in_=_i),
                   (wl_tok[ws],), ("GQ", 16))
            tbs = list(range(NBLK)) if kind in (0, 3) else list(range(3, NBLK))
            job_ev = {}
            for tb in tbs:
                pslot = tb % 2
                mm = None
                for ic in range(8):
                    mov = y1v[:, ic, tb * 512:(tb + 1) * 512]
                    mm = S("tensor", lambda e, _s=wcol[ws][:, ic * 128:(ic + 1) * 128], _m=mov,
                           _o=kqps[pslot][:], _st=(ic == 0), _sp=(ic == 7):
                           e.matmul(_o, _s, _m, start=_st, stop=_sp),
                           (wl, job_ev.get(tb - 2), prev_job_last_ev), ("TS", 1))
                if kind == 0:
                    if tb >= 3:
                        evo = kfm[:, oc * KW + tb * 512 - KB0: oc * KW + (tb + 1) * 512 - KB0]
                        ev = S("scalar", lambda e, _o=evo, _i=kqps[pslot][:], _b=kbcol[:, oc:oc + 1]:
                               e.activation(_o, _i, AF.Identity, bias=_b), (mm,), ("SS", 1))
                    elif tb < 2:
                        evo = kearly[:, oc * KB0 + tb * 512: oc * KB0 + (tb + 1) * 512]
                        ev = S("scalar", lambda e, _o=evo, _i=kqps[pslot][:], _b=kbcol[:, oc:oc + 1]:
                               e.activation(_o, _i, AF.Identity, bias=_b), (mm,), ("SS", 1))
                    else:
                        S("scalar", lambda e, _o=kearly[:, oc * KB0 + 1024: oc * KB0 + 1280],
                          _i=kqps[pslot][:, 0:256], _b=kbcol[:, oc:oc + 1]:
                          e.activation(_o, _i, AF.Identity, bias=_b), (mm,), ("SS", 1))
                        ev = S("scalar", lambda e, _o=kfm[:, oc * KW: oc * KW + 256],
                               _i=kqps[pslot][:, 256:512], _b=kbcol[:, oc:oc + 1]:
                               e.activation(_o, _i, AF.Identity, bias=_b), (mm,), ("SS", 1))
                    last_k_ev = ev
                elif kind == 3:
                    vs_ = tb % 2
                    ev = S("scalar", lambda e, _o=vtmp[vs_][:], _i=kqps[pslot][:], _sc=ifgcol[:, oc:oc + 1]:
                           e.activation(_o, _i, AF.Identity, scale=_sc), (mm, vtp_tok.get((ji, tb - 1)), vtp_tok.get((ji - 1, NBLK - 1))), ("SS", 1))
                    tps = None
                    for ts4 in range(4):
                        tps = S("tensor", lambda e, _o=trpv[vs_][:, ts4 * 128:(ts4 + 1) * 128],
                                _i=vtmp[vs_][:, ts4 * 128:(ts4 + 1) * 128]:
                                e.transpose(_o, _i, idb[:]), (ev, vev_tok.get((ji, tb - 2))), ("TS", 1))
                    vtp_tok[(ji, tb)] = tps
                    vdst = bass.AP(vtok[:].tensor, (tb * 4) * 1024 + oc * 128,
                                   [[NT * 1024, 128], [1024, 4], [1, 128]])
                    vev = S("vector", lambda e, _o=vdst,
                            _i=trpv[vs_][:].rearrange("p (a j) -> p a j", a=4):
                            e.tensor_copy(_o, _i), (tps,), ("VS", 1))
                    vev_tok[(ji, tb)] = vev
                    ev = vev
                else:
                    tbo = tb - 3
                    qs_ = tbo % 2
                    bias = qbcol[:, oc:oc + 1] if kind == 1 else gbcol[:, oc:oc + 1]
                    fn = AF.Identity if kind == 1 else AF.Sigmoid
                    ev = S("scalar", lambda e, _o=qtmp[qs_][:], _i=kqps[pslot][:], _b=bias, _f=fn:
                           e.activation(_o, _i, _f, bias=_b), (mm, sp_tok.get((ji, tbo - 1)), sp_tok.get((ji - 1, 3)),
                                                              qtp_tok.get((ji, tbo - 2))), ("SS", 1))
                    img = t["q_img"] if kind == 1 else t["gate_img"]
                    sp = S("sync", lambda e, _i=qtmp[qs_][:],
                           _o=bass.AP(img, oc * 128 * TOK + tbo * 512, [[TOK, 128], [1, 512]]):
                           e.dma_start(out=_o, in_=_i), (ev,), ("SQ", 16))
                    sp_tok[(ji, tbo)] = sp
                    if kind == 1:
                        tpq = None
                        for a4 in range(4):
                            tpq = S("tensor", lambda e, _o=trpk[0][:, a4 * 128:(a4 + 1) * 128],
                                    _i=qtmp[qs_][:, a4 * 128:(a4 + 1) * 128]:
                                    e.transpose(_o, _i, idb[:]),
                                    (ev, last_qts_ev), ("TS", 1))
                        qtp_tok[(ji, tbo)] = tpq
                        qeng = "vector" if (oc + tbo) % 2 == 0 else "scalar"
                        qcp = (lambda e: e.tensor_copy(qts[:], trpk[0][:, 0:512]))                             if qeng == "vector" else                               (lambda e: e.copy(qts[:], trpk[0][:, 0:512]))
                        last_qts_ev = S(qeng, qcp, (tpq, last_qts_dma),
                                        ("VS" if qeng == "vector" else "SS", 1))
                        last_qts_dma = S("sync", lambda e,
                                         _o=bass.AP(t["qtok"], (tbo * 4) * 128 * 1024 + oc * 128,
                                                    [[1024, 128], [128 * 1024, 4], [1, 128]]),
                                         _i=bass.AP(qts[:].tensor, 0, [[512, 128], [128, 4], [1, 128]]):
                                         e.dma_start(out=_o, in_=_i),
                                         (last_qts_ev,), ("SQ", 16), batch=("qtw", (oc, tbo)))
                job_ev[tb] = ev
            prev_job_last_ev = job_ev[tbs[-1]]
            wl_tok[ws] = prev_job_last_ev
        ktv_ev = {}
        ktv_dma = {}
        for T in range(NKT):
            tslot = T % 2
            tps = None
            for oc in range(8):
                if T >= 10:
                    src = kfm[:, oc * KW + T * 128 - KB0: oc * KW + (T + 1) * 128 - KB0]
                else:
                    src = kearly[:, oc * KB0 + T * 128: oc * KB0 + (T + 1) * 128]
                tps = S("tensor", lambda e, _o=trpk[tslot][:, oc * 128:(oc + 1) * 128], _i=src:
                        e.transpose(_o, _i, idb[:]),
                        (last_k_ev, last_qts_ev, ktv_ev.get(T - 2)), ("TS", 1))
            eng = "vector" if T % 2 == 0 else "scalar"
            cpf = (lambda e, _o=wcol[tslot][:], _i=trpk[tslot][:]: e.tensor_copy(_o, _i)) \
                if eng == "vector" else \
                  (lambda e, _o=wcol[tslot][:], _i=trpk[tslot][:]: e.copy(_o, _i))
            ev = S(eng, cpf, (tps, prev_job_last_ev, ktv_dma.get(T - 2)),
                   ("VS" if eng == "vector" else "SS", 1))
            ktv_ev[T] = ev
            ktv_dma[T] = S("sync", lambda e, _o=bass.AP(t["ktok"], T * 128 * 1024, [[1024, 128], [1, 1024]]),
                           _i=wcol[tslot][:]: e.dma_start(out=_o, in_=_i),
                           (ev,), ("SQ", 16), batch=("ktw", T))
        sched.barrier()

    if DEBUG:
        S("sync", lambda e: e.dma_start(out=t["dbg_kfm"].ap(), in_=kfm[:]), (), ("SQ", 16))
        S("sync", lambda e: e.dma_start(out=t["dbg_vt"].ap(), in_=vtok[:]), (), ("SQ", 16))

    es1y.close()

    es2 = contextlib.ExitStack()
    pspsb = sb("pspsb", [128, NB * 2048], BF16, es2)
    band = [sb(f"band{i}", [128, CW], BF16, es2) for i in range(2)]
    qsb = [sb(f"qsb{i}", [128, CW], BF16, es2) for i in range(2)]
    qsl = [sb(f"qsl{i}", [128, 1024], BF16, es2) for i in range(2)]
    pbmb = [sb(f"pbmb{i}", [128, CW], BF16, es2) for i in range(2)]
    rbuf = [sb(f"rbuf{i}", [128, 8 * 208], BF16, es2) for i in range(2)]
    t2b = sb("t2b", [128, CW], F32, es2)
    Pb = [sb(f"Pb{i}", [128, CW], BF16, es2) for i in range(2)]
    Zb = sb("Zb", [128, 16], F32, es2)
    rb = sb("rb", [128, 16], F32, es2)
    ktr = sb("ktr", [128, RING * 1024], BF16, es2)
    qtb = sb("qtb", [128, 1024], BF16, es2)
    scr = sb("scr", [128, 1024], BF16, es2)
    acch = sb("acch", [128, 1024], BF16, es2)
    accTb = sb("accTb", [128, 1024], BF16, es2)
    ptmp = sb("ptmp", [128, 1024], BF16, es2)
    pvv = sb("pvv", [128, 1024], BF16, es2)
    ptv = sb("ptv", [128, 1024], BF16, es2)
    bandx = sb("bandx", [128, 80], F32, es2)
    genps = psum("genps", [128, 2048], F32, es2)
    pvps = [psum(f"pvps{i}", [128, 512], F32, es2) for i in range(2)]
    pvpsB = [psum(f"pvpsB{i}", [128, 512], F32, es2) for i in range(2)]

    qsl_tok = [None, None]; pbm_tok = [None, None]
    gen_mm = {}; gen_ev = {}; gen_free = {}; rw_tok = {}; brd_tok = {}
    psp_rd = {}; psp_wr = {}; pv_tok = {}; pvev = {}
    P_tok = {}; exp_tok = {}; sm2 = {}
    band_used = {}; qs_used = {}; pbm_used = {}
    last_gen = {}
    qtb_ev = {}; ew_lastmul = {}; ew_red = {}; ring_ld = {}
    pve_last = {}; pve_cmb = {}; accT_tok = {}; merge_tok = {}

    for T in range(RING):
        ring_ld[T] = S("gpsimd", lambda e, _o=ktr[:, (T % RING) * 1024:(T % RING + 1) * 1024],
                       _i=bass.AP(t["ktok"], T * 128 * 1024, [[1024, 128], [1, 1024]]):
                       e.dma_start(out=_o, in_=_i), (), ("GQ", 16), batch=("kr", -1))

    def stage_C_hp(qt, hp):
        pslot = hp % 2
        mm = None
        for ii, ib in enumerate(range(NB)):
            tile_ib = 12 + qt + BO[ib]
            for ab in range(2):
                h = 2 * hp + ab
                stat = vtok[:, tile_ib * 1024 + h * 64: tile_ib * 1024 + (h + 1) * 64]
                mov = bass.AP(pspsb[:].tensor, ib * 2048 + h, [[NB * 2048, 128], [16, 128]])
                outap = pvps[pslot][0:64, 0:128] if ab == 0 else pvpsB[pslot][64:128, 0:128]
                mm = S("tensor", lambda e, _o=outap, _s=stat, _m=mov, _st=(ii == 0), _sp=(ii == NB - 1),
                       _tp=((0, 0) if ab == 0 else (0, 64)):
                       e.matmul(_o, _s, _m, start=_st, stop=_sp, tile_position=_tp),
                       (psp_rd.get(qt), pvev.get((qt, hp - 2)), pvev.get((qt - 1, hp + 6))), ("TS", 1))
        pv_tok[(qt, hp)] = mm
        ev = S("scalar", lambda e, _o=outfl[0:64, hp * TOK + qt * 128: hp * TOK + (qt + 1) * 128],
               _i=pvps[pslot][0:64, 0:128], _b=bvifg[0:64, hp:hp + 1]:
               e.activation(_o, _i, AF.Identity, bias=_b), (mm,), ("SS", 1))
        ev = S("scalar", lambda e, _o=outfl[64:128, hp * TOK + qt * 128: hp * TOK + (qt + 1) * 128],
               _i=pvpsB[pslot][64:128, 0:128], _b=bvifg[64:128, hp:hp + 1]:
               e.activation(_o, _i, AF.Identity, bias=_b), (mm,), ("SS", 1))
        pvev[(qt, hp)] = ev
        merge_tok[qt] = S("vector", lambda e,
                          _o=outfl[:, hp * TOK + qt * 128: hp * TOK + (qt + 1) * 128],
                          _a=accTb[:, hp * 128:(hp + 1) * 128]:
                          e.tensor_tensor(_o, _o, _a, ALU.add),
                          (ev, accT_tok[qt]), ("VS", 1))

    def stage_A(qt):
        n0 = HALO + qt * 128
        slot = qt % 2
        qsl_tok[slot] = S("scalar", lambda e, _o=qsl[slot][:],
                          _i=bass.AP(t["q_img"], qt * 128, [[TOK, 128], [128 * TOK, 8], [1, 128]]):
                          e.dma_start(out=_o, in_=_i), (last_gen.get(qt - 2),), ("AQ", 16), batch=("ld", qt))
        pbm_tok[slot] = S("scalar", lambda e, _o=pbmb[slot][:],
                          _i=bass.AP(t["pbm"], qt * 128 * CW, [[CW, 128], [1, CW]]):
                          e.dma_start(out=_o, in_=_i), (pbm_used.get(qt - 2),), ("AQ", 16), batch=("ld", qt))
        qtb_ev[qt] = S("scalar", lambda e, _o=qtb[:],
                       _i=bass.AP(t["qtok"], qt * 128 * 1024, [[1024, 128], [1, 1024]]):
                       e.dma_start(out=_o, in_=_i),
                       (ew_lastmul.get(qt - 1),), ("AQ", 16), batch=("ld", qt))
        genidx = 0
        rwidx = 0
        pvi = [0]
        for gi, (w0o, w, sb_, ns, ds) in enumerate(GROUPS):
            w0 = n0 + w0o
            for quarter in range(4):
                half = quarter // 2
                pbank = (genidx % 2) * 1024
                mm = None
                for hs in range(4):
                    h = quarter * 4 + hs
                    hp, r = h // 2, h % 2
                    reg = pbank + (hs % 2) * 512 + (hs // 2) * 256
                    stat = qsl[slot][r * 64:(r + 1) * 64, hp * 128:(hp + 1) * 128]
                    mov = kfm[r * 64:(r + 1) * 64, hp * KW + w0 - KB0: hp * KW + w0 - KB0 + w]
                    waits = [qsl_tok[slot], gen_free.get((qt, genidx - 2)),
                             (gen_free.get((qt - 1, NGEN4 - 2 + (genidx % 2))) if genidx < 2 else None)]
                    mm = S("tensor", lambda e, _o=genps[:, reg:reg + w], _s=stat, _m=mov, _tp=(r * 64, 0):
                           e.matmul(_o, _s, _m, start=True, stop=True, tile_position=_tp), waits, ("TS", 1))
                    if gi == 0:
                        mm = S("tensor", lambda e, _o=genps[:, reg + 160:reg + 160 + NSLOT], _s=stat,
                               _m=se2b[r * 64:(r + 1) * 64, :], _tp=(r * 64, 0):
                               e.matmul(_o, _s, _m, start=True, stop=True, tile_position=_tp),
                               (c1,), ("TS", 1))
                gen_mm[(qt, genidx)] = mm
                if qt >= 2 and genidx % 3 != 2 and pvi[0] < 8:
                    stage_C_hp(qt - 2, pvi[0])
                    pvi[0] += 1
                eng = "vector" if genidx % 2 == 0 else "scalar"
                esem = "VS" if eng == "vector" else "SS"
                rslot = rwidx % 2
                pin = bass.AP(genps[:].tensor, pbank, [[2048, 128], [256, 2], [512, 2], [1, w]])
                pout = bass.AP(rbuf[rslot][:].tensor, (quarter % 2) * 4, [[8 * 208, 128], [2, 2], [1, 2], [8, w]])
                cpf = (lambda e, _o=pout, _i=pin: e.tensor_copy(_o, _i)) if eng == "vector" else \
                      (lambda e, _o=pout, _i=pin: e.copy(_o, _i))
                ev = S(eng, cpf,
                       (mm, rw_tok.get((qt, rwidx - 2)), rw_tok.get((qt - 1, rwidx + 4))), (esem, 1))
                gen_ev[(qt, genidx)] = ev
                freetok = ev
                if gi == 0:
                    qi = bass.AP(genps[:].tensor, pbank + 160, [[2048, 128], [256, 2], [512, 2], [1, NSLOT]])
                    qo = bass.AP(qsb[slot][:].tensor, quarter * 4, [[CW, 128], [2, 2], [1, 2], [16, NSLOT]])
                    qcpf = (lambda e, _o=qo, _i=qi: e.tensor_scalar(_o, _i, 1.0, None, ALU.add)) \
                        if eng == "vector" else \
                           (lambda e, _o=qo, _i=qi: e.activation(_o, _i, AF.Identity, bias=1.0))
                    freetok = S(eng, qcpf, (mm, qs_used.get(qt - 2)), (esem, 1))
                gen_free[(qt, genidx)] = freetok
                genidx += 1
                if quarter % 2 == 1:
                    roff = ((qt % 2) * 2 + half) * 128 * (8 * w + 8)
                    rw = S("sync", lambda e, _i=rbuf[rslot][:, 0:8 * w],
                           _o=bass.AP(t[f"rimg{gi}"], roff, [[8 * w + 8, 128], [1, 8 * w]]):
                           e.dma_start(out=_o, in_=_i),
                           (gen_free[(qt, genidx - 1)], gen_free[(qt, genidx - 2)],
                            gen_ev[(qt, genidx - 1)], gen_ev[(qt, genidx - 2)],
                            brd_tok.get((qt - 2, rwidx))), ("SQ", 16),
                           batch=("rw", qt))
                    rw_tok[(qt, rwidx)] = rw
                    rwidx += 1
        last_gen[qt] = gen_mm[(qt, NGEN4 - 1)]
        red = ew_red.get(qt - 1)
        mul = None
        for sl, dlt in EWS:
            tile = qt + dlt
            mul = S("vector", lambda e, _k=ktr[:, (tile % RING) * 1024:((tile % RING) + 1) * 1024]:
                    e.tensor_tensor(scr[:], qtb[:], _k, ALU.mult),
                    (qtb_ev[qt], ring_ld[tile], red), ("VS", 1))
            red = S("vector", lambda e, _o=bandx[:, (sl - 41) * 16:(sl - 40) * 16]:
                    e.tensor_reduce(_o, scr[:].rearrange("p (hh d) -> p hh d", hh=16),
                                    mybir.AxisListType.X, ALU.add), (mul,), ("VS", 1))
        ew_lastmul[qt] = mul
        ew_red[qt] = red

    def stage_B1(qt):
        slot = qt % 2
        brd = []
        rwidx = 0
        for gi, (w0o, w, sb_, ns, ds) in enumerate(GROUPS):
            for half in range(2):
                roff = ((qt % 2) * 2 + half) * 128 * (8 * w + 8)
                if gi == 0:
                    src = bass.AP(t[f"rimg{gi}"], roff, [[8 * w + 16, 128], [1, 33 * 8]])
                    dst = bass.AP(band[slot][:].tensor, half * 8, [[CW, 128], [16, 33], [1, 8]])
                else:
                    src = bass.AP(t[f"rimg{gi}"], roff, [[8 * w + 16, 128], [8 * ds, ns], [1, 8]])
                    dst = bass.AP(band[slot][:].tensor, sb_ * 16 + half * 8, [[CW, 128], [16, ns], [1, 8]])
                brd.append(S("scalar", lambda e, _o=dst, _i=src: e.dma_start(out=_o, in_=_i),
                             (rw_tok[(qt, rwidx)], band_used.get(qt - 2)), ("AQ", 16), batch=("brd", qt)))
                brd_tok[(qt, rwidx)] = brd[-1]
                rwidx += 1
        S("vector", lambda e: e.tensor_copy(band[slot][:, 656:736], bandx[:]),
          (band_used.get(qt - 2), ew_red[qt]), ("VS", 1))
        v1 = S("vector", lambda e: e.tensor_tensor(t2b[:], band[slot][:], qsb[slot][:], ALU.mult),
               tuple(brd) + (gen_free[(qt, 0)], gen_free[(qt, 1)], gen_free[(qt, 2)], gen_free[(qt, 3)], P_tok.get(qt - 1)), ("VS", 1))
        qs_used[qt] = v1
        band_used[qt] = v1
        v3 = S("vector", lambda e: e.scalar_tensor_tensor(t2b[:], t2b[:], 0.125, pbmb[slot][:], ALU.mult, ALU.add),
               (v1, pbm_tok[slot], exp_tok.get(qt - 1)), ("VS", 1))
        pbm_used[qt] = v3
        s1 = S("scalar", lambda e: e.activation(t2b[:], t2b[:], AF.Exp), (v3,), ("SS", 1))
        exp_tok[qt] = s1
        v4 = S("vector", lambda e: e.tensor_reduce(Zb[:], t2b[:].rearrange("p (s hh) -> p hh s", hh=16),
                                                   mybir.AxisListType.X, ALU.add), (s1,), ("VS", 1))
        v5 = S("vector", lambda e: e.reciprocal(rb[:], Zb[:]), (v4,), ("VS", 1))
        rbc = bass.AP(rb[:].tensor, 0, [[16, 128], [0, NSLOT], [1, 16]])
        Ev3 = t2b[:].rearrange("p (s hh) -> p s hh", hh=16)
        Pv3 = Pb[slot][:].rearrange("p (s hh) -> p s hh", hh=16)
        v6 = S("vector", lambda e, _r=rbc, _e=Ev3, _p=Pv3: e.tensor_tensor(_p, _e, _r, ALU.mult),
               (v5, psp_wr.get(qt - 2), pve_cmb.get(qt - 2)), ("VS", 1))
        P_tok[qt] = v6
        ibase = (qt % 2) * NB * 128 * 2048
        wr = []
        wr.append(S("sync", lambda e, _i=Pb[slot][:].rearrange("p (j hh) -> p j hh", hh=16)[:, 0:33, :],
                    _o=bass.AP(t["psp_img"], ibase + LOC_ROW0 * 2048, [[2064, 128], [2048, 33], [1, 16]]):
                    e.dma_start(out=_o, in_=_i),
                    (v6, psp_rd.get((qt - 2, 0)), psp_rd.get((qt - 2, 1))), ("SQ", 16)))
        for sl in range(33, 41):
            if not SLOT_REAL[sl]:
                continue
            o = SLOT_OFF[sl]
            wr.append(S("sync", lambda e, _i=Pb[slot][:, sl * 16:(sl + 1) * 16],
                        _o=bass.AP(t["psp_img"], ibase + DY_ROW0[o] * 2048, [[2064, 128], [1, 16]]):
                        e.dma_start(out=_o, in_=_i), (v6,), ("SQ", 16), batch=("pw", qt)))
        psp_wr[qt] = wr[-1]
        tile = qt + RING
        if tile < NKT:
            ring_ld[tile] = S("gpsimd", lambda e,
                              _o=ktr[:, (tile % RING) * 1024:((tile % RING) + 1) * 1024],
                              _i=bass.AP(t["ktok"], tile * 128 * 1024, [[1024, 128], [1, 1024]]):
                              e.dma_start(out=_o, in_=_i), (ew_lastmul[qt],), ("GQ", 16), batch=("kr", qt))
        if DEBUG:
            S("sync", lambda e, _o=bass.AP(t["dbg_band"], qt * 128 * CW, [[CW, 128], [1, CW]]),
              _i=band[slot][:]: e.dma_start(out=_o, in_=_i), (v2,), ("SQ", 16))
            S("sync", lambda e, _o=bass.AP(t["dbg_qs"], qt * 128 * CW, [[CW, 128], [1, CW]]),
              _i=qsb[slot][:]: e.dma_start(out=_o, in_=_i), (v1,), ("SQ", 16))
            S("sync", lambda e, _o=bass.AP(t["dbg_lg"], qt * 128 * CW, [[CW, 128], [1, CW]]),
              _i=t2b[:]: e.dma_start(out=_o, in_=_i), (exp_tok[qt],), ("SQ", 16))
            S("sync", lambda e, _o=bass.AP(t["dbg_P"], qt * 128 * CW, [[CW, 128], [1, CW]]),
              _i=Pb[slot][:]: e.dma_start(out=_o, in_=_i), (P_tok[qt],), ("SQ", 16))
            if qt == 4:
                S("sync", lambda e, _o=bass.AP(t["dbg_psp"], 0, [[2048, NB * 128], [1, 2048]]),
                  _i=bass.AP(t["psp_img"], (qt % 2) * NB * 128 * 2048, [[2048, NB * 128], [1, 2048]]):
                  e.dma_start(out=_o, in_=_i), (psp_rd.get(qt, wr[-1]),), ("SQ", 16))
    def stage_B2(qt):
        ibase = (qt % 2) * NB * 128 * 2048
        rd = None
        for ib in range(NB):
            src = bass.AP(t["psp_img"], ibase + ib * 128 * 2048, [[2048, 128], [1, 2048]])
            dst = pspsb[:, ib * 2048:(ib + 1) * 2048]
            rd = S("scalar", lambda e, _o=dst, _i=src: e.dma_start(out=_o, in_=_i),
                   (psp_wr[qt], pv_tok.get((qt - 1, 7))), ("AQ", 16), batch=("pr", qt))
        psp_rd[qt] = rd
        slot = qt % 2
        m = None
        vlast = None
        for i, (sl, dlt) in enumerate(EWS):
            tile = qt + dlt
            pbc = bass.AP(Pb[slot][:].tensor, sl * 16, [[CW, 128], [1, 16], [0, 64]])
            vsl = vtok[:, tile * 1024:(tile + 1) * 1024]
            if i < 3:
                if i == 0:
                    m = S("gpsimd", lambda e, _v=vsl, _p=pbc:
                          e.tensor_tensor(acch[:], _v, _p, ALU.mult),
                          (P_tok[qt], accT_tok.get(qt - 1)), ("GS", 1))
                else:
                    m = S("gpsimd", lambda e, _v=vsl, _p=pbc:
                          e.tensor_tensor(ptmp[:], _v, _p, ALU.mult), (), ("GS", 1))
                    m = S("gpsimd", lambda e: e.tensor_tensor(acch[:], acch[:], ptmp[:], ALU.add),
                          (), ("GS", 1))
            else:
                if i == 3:
                    vlast = S("vector", lambda e, _v=vsl, _p=pbc:
                              e.tensor_tensor(pvv[:], _v, _p, ALU.mult),
                              (P_tok[qt], accT_tok.get(qt - 1)), ("VS", 1))
                else:
                    vlast = S("vector", lambda e, _v=vsl, _p=pbc:
                              e.tensor_tensor(ptv[:], _v, _p, ALU.mult), (), ("VS", 1))
                    vlast = S("vector", lambda e: e.tensor_tensor(pvv[:], pvv[:], ptv[:], ALU.add),
                              (), ("VS", 1))
        pve_last[qt] = vlast
        pve_cmb[qt] = None
        cst = S("gpsimd", lambda e: e.tensor_tensor(acch[:], acch[:], pvv[:], ALU.add),
                (m, vlast), ("GS", 1))
        pve_cmb[qt] = cst
        accT_tok[qt] = S("sync", lambda e: e.dma_start(
            out=accTb[:].rearrange("p (m t) -> p m t", m=8), in_=acch[:], transpose=True),
            (cst, merge_tok.get(qt - 2)), ("SQ", 16), batch=("pvt", qt))

    for it in range(NQT + 2):
        if 1 <= it <= NQT:
            stage_B1(it - 1)
        if it < NQT:
            stage_A(it)
        if it >= 2 and it >= NQT:
            for hp in range(8):
                stage_C_hp(it - 2, hp)
        if 1 <= it <= NQT:
            stage_B2(it - 1)
    sched.barrier()
    sched.barrier()
    es2.close()
    es1.close()

    if DEBUG:
        S("sync", lambda e: e.dma_start(out=t["dbg_of"].ap(), in_=outfl[:]), (), ("SQ", 16))

    with contextlib.ExitStack() as e2b:
        wo_sb = sb("wo_sb", [128, 8 * 1024], BF16, e2b)
        gated = sb("gated", [128, 8 * TOK], BF16, e2b)
        gsl = [sb(f"gsl{i}", [128, TOK], BF16, e2b) for i in range(2)]
        apj = sb("apj", [128, 8 * TOK], BF16, e2b)
        atile = [sb(f"atile{i}", [128, 1024], BF16, e2b) for i in range(2)]
        xres = [sb(f"xres{i}", [128, 1024], F32, e2b) for i in range(2)]
        x1t = [sb(f"x1t{i}", [128, 1024], F32, e2b) for i in range(2)]
        lnt = [sb(f"lnt{i}", [128, 1024], F32, e2b) for i in range(2)]
        sumc2 = sb("sumc2", [128, 1], F32, e2b)
        negm2 = sb("negm2", [128, 1], F32, e2b)
        ssq2 = sb("ssq2", [128, 1], F32, e2b)
        rstd2 = sb("rstd2", [128, 1], F32, e2b)
        nmr2 = sb("nmr2", [128, 1], F32, e2b)
        sqs2 = sb("sqs2", [128, 1024], F32, e2b)
        opps = [psum(f"opps{i}", [128, 1024], F32, e2b) for i in range(2)]
        trp2 = [psum(f"trp2{i}", [128, 1024], BF16, e2b) for i in range(2)]

        wol = []
        for c in range(8):
            src = bass.AP(t["woT"], (c * 128) * 1024, [[1024, 128], [1, 1024]])
            wol.append(S("gpsimd", lambda e, _o=wo_sb[:, c * 1024:(c + 1) * 1024], _i=src:
                         e.dma_start(out=_o, in_=_i), (), ("GQ", 16)))
        gts = []
        gl_tok = [None, None]
        for c in range(8):
            gslt = c % 2
            gl = S("scalar", lambda e, _o=gsl[gslt][:],
                   _i=bass.AP(t["gate_img"], c * 128 * TOK, [[TOK, 128], [1, TOK]]):
                   e.dma_start(out=_o, in_=_i), (gl_tok[gslt],), ("AQ", 16))
            gt = S("vector", lambda e, _o=gated[:, c * TOK:(c + 1) * TOK],
                   _i0=outfl[:, c * TOK:(c + 1) * TOK], _i1=gsl[gslt][:]:
                   e.tensor_tensor(_o, _i0, _i1, ALU.mult), (gl,), ("VS", 1))
            gl_tok[gslt] = gt
            gts.append(gt)
        opev = {}
        for dc in range(8):
            for tb in range(2):
                pslot = (dc * 2 + tb) % 2
                mm = None
                for c in range(8):
                    for hn in range(2):
                        mov = gated[:, c * TOK + tb * 1024 + hn * 512: c * TOK + tb * 1024 + (hn + 1) * 512]
                        mm = S("tensor", lambda e, _o=opps[pslot][:, hn * 512:(hn + 1) * 512],
                               _s=wo_sb[:, c * 1024 + dc * 128: c * 1024 + (dc + 1) * 128], _m=mov,
                               _st=(c == 0), _sp=(c == 7):
                               e.matmul(_o, _s, _m, start=_st, stop=_sp),
                               (gts[c], wol[c], opev.get((dc, tb - 2)) or opev.get((dc - 1, tb))), ("TS", 1))
                ev = S("scalar", lambda e, _o=apj[:, dc * TOK + tb * 1024: dc * TOK + (tb + 1) * 1024],
                       _i=opps[pslot][:], _b=obcol[:, dc:dc + 1]:
                       e.activation(_o, _i, AF.Identity, bias=_b), (mm,), ("SS", 1))
                opev[(dc, tb)] = ev
        prev_sp = [None, None]
        prev_ev3 = [None, None]
        for tt in range(NQT):
            bs = tt % 2
            xr = S("scalar", lambda e, _o=xres[bs][:],
                   _i=bass.AP(t["xs"], (HALO + tt * 128) * D, [[D, 128], [1, D]]):
                   e.dma_start(out=_o, in_=_i), (prev_sp[bs],), ("AQ", 16))
            tps = None
            for dc in range(8):
                tps = S("tensor", lambda e, _o=trp2[bs][:, dc * 128:(dc + 1) * 128],
                        _i=apj[:, dc * TOK + tt * 128: dc * TOK + (tt + 1) * 128]:
                        e.transpose(_o, _i, idb[:]),
                        (opev[(dc, tt // 8)], prev_ev3[bs]), ("TS", 1))
            ev = S("vector", lambda e, _o=atile[bs][:], _i=trp2[bs][:]: e.tensor_copy(_o, _i), (tps,), ("VS", 1))
            prev_ev3[bs] = ev
            v1 = S("vector", lambda e, _x=atile[bs][:]: e.tensor_reduce(sumc2[:], _x, mybir.AxisListType.X, ALU.add), (ev,), ("VS", 1))
            v2 = S("vector", lambda e: e.tensor_scalar(negm2[:], sumc2[:], -1.0 / D, None, ALU.mult), (v1,), ("VS", 1))
            s1 = S("scalar", lambda e, _x=atile[bs][:]: e.activation(sqs2[:], _x, AF.Square, bias=negm2[:], scale=1.0, accum_out=ssq2[:]), (v2,), ("SS", 1))
            s2a = S("scalar", lambda e: e.activation(rstd2[:], ssq2[:], AF.Sqrt, bias=epsc[:], scale=1.0 / D), (s1,), ("SS", 1))
            s2 = S("vector", lambda e: e.reciprocal(rstd2[:], rstd2[:]), (s2a,), ("VS", 1))
            v3 = S("vector", lambda e: e.tensor_mul(nmr2[:], negm2[:], rstd2[:]), (s2,), ("VS", 1))
            s3 = S("scalar", lambda e, _x=atile[bs][:], _y=lnt[bs][:]:
                   e.activation(_y, _x, AF.Identity, bias=nmr2[:], scale=rstd2[:]), (v3,), ("SS", 1))
            v7 = S("vector", lambda e, _o=x1t[bs][:], _a=xres[bs][:], _b=lnt[bs][:]:
                   e.tensor_tensor(_o, _a, _b, ALU.add), (s3, xr), ("VS", 1))
            sp = S("sync", lambda e, _i=x1t[bs][:],
                   _o=bass.AP(t["x1_img"], tt * 128 * D, [[D, 128], [1, D]]):
                   e.dma_start(out=_o, in_=_i), (v7,), ("SQ", 16))
            prev_sp[bs] = sp
            if DEBUG:
                S("sync", lambda e, _i=x1t[bs][:],
                  _o=bass.AP(t["dbg_x1"], tt * 128 * D, [[D, 128], [1, D]]):
                  e.dma_start(out=_o, in_=_i), (v7,), ("SQ", 16))
        sched.barrier()

    esOF.close()

    with contextlib.ExitStack() as e3:
        wbig = sb("wbig", [128, 8 * 1024], BF16, e3)
        wscr = sb("wscr", [128, 8 * 512], F32, e3)
        g1rep = [sb(f"g1rep{i}", [128, 32], F32, e3) for i in range(2)]
        g2rep = [sb(f"g2rep{i}", [128, 16], F32, e3) for i in range(2)]
        tbuf = sb("tbuf", [128, 512], F32, e3)
        x2fm = sb("x2fm", [128, 8 * TOK], BF16, e3)
        ffm = sb("ffm", [128, 8 * TOK], BF16, e3)
        x1r = [sb(f"x1r{i}", [128, 1024], F32, e3) for i in range(2)]
        x2t = [sb(f"x2t{i}", [128, 1024], BF16, e3) for i in range(2)]
        ftile = [sb(f"ftile{i}", [128, 1024], BF16, e3) for i in range(2)]
        lnt3 = [sb(f"lnt3{i}", [128, 1024], F32, e3) for i in range(2)]
        otile = [sb(f"otile{i}", [128, 1024], F32, e3) for i in range(2)]
        sumc3 = sb("sumc3", [128, 1], F32, e3)
        negm3 = sb("negm3", [128, 1], F32, e3)
        ssq3 = sb("ssq3", [128, 1], F32, e3)
        rstd3 = sb("rstd3", [128, 1], F32, e3)
        nmr3 = sb("nmr3", [128, 1], F32, e3)
        sqs3 = sb("sqs3", [128, 1024], F32, e3)
        fps = [psum(f"fps{i}", [128, 1024], F32, e3) for i in range(2)]
        trp3 = [psum(f"trp3{i}", [128, 1024], BF16, e3) for i in range(2)]

        x2v = x2fm[:].rearrange("p (c tq) -> p c tq", c=8)
        wb_tok = {}
        gtoks = [None, None]
        for ri, (g1n, g2n) in enumerate((("g1rh", "g2rh"), ("g1ih", "g2ih"))):
            for ic in range(4):
                acc_tok = None
                for c in range(4):
                    gslt = c % 2
                    l1 = S("gpsimd", lambda e, _o=g1rep[gslt][:],
                           _i=bass.AP(t[g1n], (c * 4 + ic) * 128 * 32, [[32, 128], [1, 32]]):
                           e.dma_start(out=_o, in_=_i), (gtoks[gslt],), ("GQ", 16))
                    l2 = S("gpsimd", lambda e, _o=g2rep[gslt][:],
                           _i=bass.AP(t[g2n], c * 128 * 16, [[16, 128], [1, 16]]):
                           e.dma_start(out=_o, in_=_i), (gtoks[gslt],), ("GQ", 16))
                    g1ap = bass.AP(g1rep[gslt][:].tensor, 0, [[32, 128], [1, 32], [0, 16]])
                    g2ap = bass.AP(g2rep[gslt][:].tensor, 0, [[16, 128], [0, 32], [1, 16]])
                    dst = wscr[:, (ri * 4 + ic) * 512:(ri * 4 + ic + 1) * 512] if c == 0 else tbuf[:]
                    mul = S("vector", lambda e, _o=dst, _a=g1ap, _b=g2ap:
                            e.tensor_tensor(_o, _a, _b, ALU.mult), (l1, l2), ("VS", 1))
                    gtoks[gslt] = mul
                    if c > 0:
                        wsl = wscr[:, (ri * 4 + ic) * 512:(ri * 4 + ic + 1) * 512]
                        acc_tok = S("vector", lambda e, _o=wsl, _b=tbuf[:]:
                                    e.tensor_tensor(_o, _o, _b, ALU.add), (mul, acc_tok), ("VS", 1))
                    else:
                        acc_tok = mul
                wb_tok[(ri, ic)] = acc_tok
        wq_tok = []
        for icg in range(8):
            half_in = icg // 4
            ic = icg % 4
            for oh in range(2):
                if half_in == 0:
                    srcw = wscr[:, (oh * 4 + ic) * 512:(oh * 4 + ic + 1) * 512]
                    sign = 1.0
                    wwait = wb_tok[(oh, ic)]
                else:
                    srcw = wscr[:, ((1 - oh) * 4 + ic) * 512:((1 - oh) * 4 + ic + 1) * 512]
                    sign = -1.0 if oh == 0 else 1.0
                    wwait = wb_tok[(1 - oh, ic)]
                wq_tok.append(S("vector", lambda e, _o=wbig[:, icg * 1024 + oh * 512: icg * 1024 + (oh + 1) * 512],
                                _i=srcw, _s=sign:
                                e.tensor_scalar(_o, _i, _s, None, ALU.mult), (wwait,), ("VS", 1)))
        wq_last = wq_tok[-1]

        prev_ap = [None, None]
        x2_tok = {}
        for tt in range(NQT):
            bs = tt % 2
            ld = S("scalar", lambda e, _o=x1r[bs][:],
                   _i=bass.AP(t["x1_img"], tt * 128 * D, [[D, 128], [1, D]]):
                   e.dma_start(out=_o, in_=_i), (prev_ap[bs],), ("AQ", 16))
            v1 = S("vector", lambda e, _x=x1r[bs][:]: e.tensor_reduce(sumc3[:], _x, mybir.AxisListType.X, ALU.add), (ld,), ("VS", 1))
            v2 = S("vector", lambda e: e.tensor_scalar(negm3[:], sumc3[:], -1.0 / D, None, ALU.mult), (v1,), ("VS", 1))
            s1 = S("scalar", lambda e, _x=x1r[bs][:]: e.activation(sqs3[:], _x, AF.Square, bias=negm3[:], scale=1.0, accum_out=ssq3[:]), (v2,), ("SS", 1))
            s2a = S("scalar", lambda e: e.activation(rstd3[:], ssq3[:], AF.Sqrt, bias=epsc[:], scale=1.0 / D), (s1,), ("SS", 1))
            s2 = S("vector", lambda e: e.reciprocal(rstd3[:], rstd3[:]), (s2a,), ("VS", 1))
            v3 = S("vector", lambda e: e.tensor_mul(nmr3[:], negm3[:], rstd3[:]), (s2,), ("VS", 1))
            s3 = S("scalar", lambda e, _x=x1r[bs][:], _y=x2t[bs][:]:
                   e.activation(_y, _x, AF.Identity, bias=nmr3[:], scale=rstd3[:]), (v3,), ("SS", 1))
            prev_ap[bs] = s3
            tps = None
            for c in range(8):
                tps = S("tensor", lambda e, _o=trp3[bs][:, c * 128:(c + 1) * 128],
                        _i=x2t[bs][:, c * 128:(c + 1) * 128]:
                        e.transpose(_o, _i, idb[:]), (s3, x2_tok.get(tt - 2)), ("TS", 1))
            ev = S("vector", lambda e, _o=x2v[:, :, tt * 128:(tt + 1) * 128],
                   _i=trp3[bs][:].rearrange("p (c tq) -> p c tq", c=8):
                   e.tensor_copy(_o, _i), (tps,), ("VS", 1))
            x2_tok[tt] = ev

        fev = {}
        for oc in range(8):
            for tb in range(2):
                pslot = (oc * 2 + tb) % 2
                mm = None
                for ic in range(8):
                    for hn in range(2):
                        mov = x2v[:, ic, tb * 1024 + hn * 512: tb * 1024 + (hn + 1) * 512]
                        mm = S("tensor", lambda e, _o=fps[pslot][:, hn * 512:(hn + 1) * 512],
                               _s=wbig[:, ic * 1024 + oc * 128: ic * 1024 + (oc + 1) * 128], _m=mov,
                               _st=(ic == 0), _sp=(ic == 7):
                               e.matmul(_o, _s, _m, start=_st, stop=_sp),
                               (wq_last, x2_tok[tb * 8 + 7], fev.get((oc, tb - 2)) or fev.get((oc - 1, tb))), ("TS", 1))
                ev = S("scalar", lambda e, _o=ffm[:, oc * TOK + tb * 1024: oc * TOK + (tb + 1) * 1024],
                       _i=fps[pslot][:], _b=fbcol[:, oc:oc + 1]:
                       e.activation(_o, _i, AF.Identity, bias=_b), (mm,), ("SS", 1))
                fev[(oc, tb)] = ev

        prev_ot = [None, None]
        prev_ev4 = [x2_tok.get(NQT - 2), x2_tok.get(NQT - 1)]
        out_toks = []
        for tt in range(NQT):
            bs = tt % 2
            ld = S("scalar", lambda e, _o=x1r[bs][:],
                   _i=bass.AP(t["x1_img"], tt * 128 * D, [[D, 128], [1, D]]):
                   e.dma_start(out=_o, in_=_i), (prev_ot[bs], prev_ap[bs]), ("AQ", 16))
            tps = None
            for oc in range(8):
                tps = S("tensor", lambda e, _o=trp3[bs][:, oc * 128:(oc + 1) * 128],
                        _i=ffm[:, oc * TOK + tt * 128: oc * TOK + (tt + 1) * 128]:
                        e.transpose(_o, _i, idb[:]), (fev[(oc, tt // 8)], prev_ev4[bs]), ("TS", 1))
            ev = S("vector", lambda e, _o=ftile[bs][:], _i=trp3[bs][:]: e.tensor_copy(_o, _i), (tps,), ("VS", 1))
            prev_ev4[bs] = ev
            v1 = S("vector", lambda e, _x=ftile[bs][:]: e.tensor_reduce(sumc3[:], _x, mybir.AxisListType.X, ALU.add), (ev,), ("VS", 1))
            v2 = S("vector", lambda e: e.tensor_scalar(negm3[:], sumc3[:], -1.0 / D, None, ALU.mult), (v1,), ("VS", 1))
            s1 = S("scalar", lambda e, _x=ftile[bs][:]: e.activation(sqs3[:], _x, AF.Square, bias=negm3[:], scale=1.0, accum_out=ssq3[:]), (v2,), ("SS", 1))
            s2a = S("scalar", lambda e: e.activation(rstd3[:], ssq3[:], AF.Sqrt, bias=epsc[:], scale=1.0 / D), (s1,), ("SS", 1))
            s2 = S("vector", lambda e: e.reciprocal(rstd3[:], rstd3[:]), (s2a,), ("VS", 1))
            v3 = S("vector", lambda e: e.tensor_mul(nmr3[:], negm3[:], rstd3[:]), (s2,), ("VS", 1))
            s3 = S("scalar", lambda e, _x=ftile[bs][:], _y=lnt3[bs][:]:
                   e.activation(_y, _x, AF.Identity, bias=nmr3[:], scale=rstd3[:]), (v3,), ("SS", 1))
            v7 = S("vector", lambda e, _o=otile[bs][:], _a=x1r[bs][:], _b=lnt3[bs][:]:
                   e.tensor_tensor(_o, _a, _b, ALU.add), (s3, ld), ("VS", 1))
            ot = S("sync", lambda e, _i=otile[bs][:],
                   _o=bass.AP(t["out"], tt * 128 * D, [[D, 128], [1, D]]):
                   e.dma_start(out=_o, in_=_i), (v7,), ("SQ", 16))
            prev_ot[bs] = ot
            out_toks.append(ot)
        S("sync", None, (out_toks[-1],), None)

    with nc.allow_non_contiguous_dma(reason="tiny strided constant loads"):
        emit(nc, sched, sems)
    es_all.close()
    return nc


def _expand_ifgcol(ifg):
    p = np.arange(128)
    hp = np.arange(8)
    return np.ascontiguousarray(ifg[(2 * hp[None, :] + p[:, None] // 64)]).astype(np.float32)


def _expand_g1(g1):
    g1 = g1.reshape(32, 4, 32)
    out = np.empty((4, 4, 128, 32), np.float32)
    p = np.arange(128)
    for c in range(4):
        for ic in range(4):
            d = ic * 8 + p // 16
            out[c, ic] = g1[:, c, d].T
    return np.ascontiguousarray(out)


def _expand_g2(g2):
    g2 = g2.reshape(4, 16, 16)
    out = np.empty((4, 128, 16), np.float32)
    p = np.arange(128)
    for c in range(4):
        out[c] = g2[c, :, p % 16]
    return np.ascontiguousarray(out)


def _build_inputs(ins):
    x = np.ascontiguousarray(np.asarray(ins["L0_attn_x"], np.float32))
    qkv_w = np.asarray(ins["L0_attn_qkv_w"], np.float32)
    gate_w = np.asarray(ins["L0_attn_gate_w"], np.float32)
    out_w = np.asarray(ins["L0_attn_out_w"], np.float32)
    se = np.asarray(ins["L0_attn_scale_embed"], np.float32)
    pb = np.asarray(ins["L0_attn_pos_bias"], np.float32)

    se_slots = np.zeros((64, NSLOT), np.float32)
    for sl in range(NSLOT):
        o = SLOT_OFF[sl] if SLOT_REAL[sl] else 48
        se_slots[:, sl] = se[OIDX[o]]
    common = dict(
        wqkvT=np.ascontiguousarray(qkv_w.T),
        wgT=np.ascontiguousarray(gate_w.T),
        woT=np.ascontiguousarray(out_w.T),
        qkvb=np.asarray(ins["L0_attn_qkv_b"], np.float32),
        gateb=np.asarray(ins["L0_attn_gate_b"], np.float32),
        outb=np.asarray(ins["L0_attn_out_b"], np.float32),
        ffnb=np.concatenate([np.asarray(ins["L0_ffn_bias_real"], np.float32),
                             np.asarray(ins["L0_ffn_bias_imag"], np.float32)]),
        se2=np.concatenate([se_slots, se_slots], axis=0),
        ifg=np.asarray(ins["L0_attn_if_gain"], np.float32),
        ifgcolh=_expand_ifgcol(np.asarray(ins["L0_attn_if_gain"], np.float32)),
        g1rh=_expand_g1(np.asarray(ins["L0_ffn_G1_R"], np.float32)),
        g2rh=_expand_g2(np.asarray(ins["L0_ffn_G2_R"], np.float32)),
        g1ih=_expand_g1(np.asarray(ins["L0_ffn_G1_I"], np.float32)),
        g2ih=_expand_g2(np.asarray(ins["L0_ffn_G2_I"], np.float32)),
        ident=np.eye(128, dtype=np.float32),
    )
    in_maps = []
    for core in range(8):
        b, qq = core // 4, core % 4
        s0 = qq * TOK
        xs = np.zeros((TOT, D), np.float32)
        src = x[b, max(s0 - HALO, 0): s0 + TOK]
        xs[TOT - src.shape[0]:] = src
        pbm = np.full((NQT, 128, NSLOT, 16), -50.0, np.float32)
        gpos = s0 + np.arange(TOK).reshape(NQT, 128)
        for sl in range(NSLOT):
            if not SLOT_REAL[sl]:
                continue
            o = SLOT_OFF[sl]
            m = gpos >= o
            pbm[:, :, sl, :] = np.where(m[:, :, None], pb[OIDX[o]][None, None, :], -50.0)
        d = dict(common)
        d["xs"] = xs
        d["pbm"] = np.ascontiguousarray(pbm.reshape(NQT, 128, CW)).astype(np.float16)
        in_maps.append(d)
    return in_maps


_prog_cache = {}


def kernel(**inputs):
    in_maps = _build_inputs(inputs)
    if "nc" not in _prog_cache:
        _prog_cache["nc"] = build_program()
    nc = _prog_cache["nc"]
    res = run_bass_kernel_spmd(nc, in_maps, core_ids=list(range(8)))
    out = np.empty((B, N, D), np.float32)
    for core in range(8):
        b, qq = core // 4, core % 4
        out[b, qq * TOK:(qq + 1) * TOK] = res.results[core]["out"]
    return out
